# version 19
# baseline (speedup 1.0000x reference)
import math
import numpy as np
import ml_dtypes
import concourse.bass as bass
import concourse.mybir as mybir
from concourse.bass_utils import run_bass_kernel_spmd
from concourse.bass_core_rust import IndirectOffsetOnAxis

F32 = mybir.dt.float32
BF16 = mybir.dt.bfloat16
I32 = mybir.dt.int32
AF = mybir.ActivationFunctionType
ALU = mybir.AluOpType
AX = mybir.AxisListType

NEG = -1.0e30


class Cfg:
    def __init__(self, B, S, D, H, E, CAP, I, NC=8):
        self.B, self.S, self.D, self.H, self.E, self.CAP, self.I, self.NC = \
            B, S, D, H, E, CAP, I, NC
        self.HD = D // H
        assert self.HD == 128
        self.T = B * S
        self.BL = B // NC
        self.TL = self.T // NC
        self.NT = self.TL // 128
        self.JT = self.T // 128
        self.DC = D // 128
        self.IC = I // 128
        self.SC = S // 128
        self.A = max(CAP // 128, 1)
        self.LA = int(math.log2(self.A))
        self.LCAP = int(math.log2(CAP))
        self.JTE = self.JT * E
        self.NB = min(512, self.TL)
        self.NBLK = self.TL // self.NB
        self.SCALE = 1.0 / math.sqrt(self.HD)
        self.PGRP = 4 if self.NT % 4 == 0 else 1
        self.SW = min(512, self.S)
        self.CW = min(512, CAP)
        self.LCW = int(math.log2(self.CW))
        self.LNC = int(math.log2(NC))
        self.NCB = max(CAP // self.CW, 1)
        self.LNCB = int(math.log2(self.NCB))
        self.QB = min(512, self.S)
        self.KCB = self.QB // 128
        self.EOFF = max(CAP, 512)


def cfg_full():
    return Cfg(B=16, S=2048, D=2048, H=16, E=16, CAP=1024, I=1408)


def cfg_small():
    return Cfg(B=8, S=256, D=256, H=2, E=16, CAP=128, I=256)


ENGS = ("sync", "vector", "scalar", "tensor", "gpsimd")


class NS:
    pass


def view(t, off, shape, dtype=BF16):
    ncols = 1
    for d in shape[1:]:
        ncols *= d
    mult = {BF16: 1, F32: 2, I32: 2}[dtype]
    assert off % 2 == 0
    ap = t.ap()[:, off:off + ncols * mult]
    if dtype != BF16:
        ap = ap.bitcast(dtype)
    if len(shape) == 3:
        ap = ap.rearrange("p (a b) -> p a b", b=shape[2])
    return ap


class P:
    def __init__(self, nc, cfg, t, cur, eng_objs):
        self.nc, self.cfg, self.t, self.cur = nc, cfg, t, cur
        self.E = eng_objs
        self.cnt = {}
        self.scratch = {}

    def run(self, eng, fname, *a, **kw):
        if self.cur == eng:
            return getattr(self.E[eng], fname)(*a, **kw)
        return None

    def inc(self, inst, sem, n):
        self.cnt[sem] = self.cnt.get(sem, 0) + n
        if inst is not None:
            inst.then_inc(self.t.sems[sem], n)
        return self.cnt[sem]

    def mark(self, sem):
        return self.cnt.get(sem, 0)

    def wait(self, eng, sem, val):
        if val is None or val <= 0:
            return
        if self.cur == eng:
            self.E[eng].wait_ge(self.t.sems[sem], val)

    def waits(self, eng, wlist):
        if wlist:
            for (s, v) in wlist:
                self.wait(eng, s, v)

    def dma(self, eng, out, in_, sem, wait=None, **kw):
        self.waits(eng, wait)
        i = self.run(eng, "dma_start", out=out, in_=in_, **kw)
        return self.inc(i, sem, 16)

    def mm(self, out, lhsT, rhs, start, stop, sem=None, wait=None):
        self.waits("tensor", wait)
        i = self.run("tensor", "matmul", out, lhsT, rhs, start=start, stop=stop)
        if sem is not None:
            return self.inc(i, sem, 1)
        return None

    def tr(self, out, in_, ident, sem=None, wait=None):
        self.waits("tensor", wait)
        i = self.run("tensor", "transpose", out, in_, ident)
        if sem is not None:
            return self.inc(i, sem, 1)
        return None

    def v(self, fname, *a, sem=None, wait=None, **kw):
        self.waits("vector", wait)
        i = self.run("vector", fname, *a, **kw)
        if sem is not None:
            return self.inc(i, sem, 1)
        return None

    def vg(self, fname, *a, wait=None, **kw):
        m = self.v(fname, *a, sem="evv", wait=wait, **kw)
        self.wait("vector", "evv", m)
        return m

    def s(self, fname, *a, sem=None, wait=None, **kw):
        self.waits("scalar", wait)
        i = self.run("scalar", fname, *a, **kw)
        if sem is not None:
            return self.inc(i, sem, 1)
        return None

    def bkw(self, key):
        return self.scratch.get(("bk", key), ("evv", 0))

    def bks(self, key, sem, mark):
        self.scratch[("bk", key)] = (sem, mark)

    def sg(self, fname, *a, wait=None, **kw):
        m = self.s(fname, *a, sem="evs", wait=wait, **kw)
        self.wait("scalar", "evs", m)
        return m

    def g(self, fname, *a, sem=None, n=16, wait=None, **kw):
        self.waits("gpsimd", wait)
        i = self.run("gpsimd", fname, *a, **kw)
        if sem is not None:
            return self.inc(i, sem, n)
        return None



def schedule(p: P):
    c = p.cfg
    t = p.t
    AR = t.AR
    nc = p.nc
    sc_ = c.SCALE

    NSM = 28
    SMB = t.ARENA - NSM * (c.JT * 2)

    o = 0
    xtile = [view(AR, o, [128, c.D], F32), view(AR, o + 2 * c.D, [128, c.D], F32)]
    o += 4 * c.D
    sq = view(AR, o, [128, c.D], F32); o += 2 * c.D
    xnf = [view(AR, o, [128, c.D], F32), view(AR, o + 2 * c.D, [128, c.D], F32)]
    o += 4 * c.D
    xnb = [view(AR, o, [128, c.D], BF16), view(AR, o + c.D, [128, c.D], BF16)]
    o += 2 * c.D
    xnch = [view(AR, o, [128, 512], F32), view(AR, o + 1024, [128, 512], F32)]
    o += 2048
    panelT = []
    for g in range(2):
        panelT.append(view(AR, o, [128, c.DC * 128 * c.PGRP], BF16))
        o += c.DC * 128 * c.PGRP
    logit_sb = view(AR, o, [128, c.NT * c.E], F32); o += c.NT * c.E * 2
    st_ = view(AR, o, [128, 16], F32); o += 32
    assert o <= SMB, ("p0", o, SMB)

    o = 0
    xnTb = view(AR, o, [128, c.DC * c.S], BF16)
    woT_sb = view(AR, o, [128, c.DC * c.D], BF16)
    o += max(c.DC * c.S, c.DC * c.D)
    wq_b, wk_b, wv_b = [], [], []
    for par in range(2):
        wq_b.append(view(AR, o, [128, c.DC * 128], BF16)); o += c.DC * 128
        wk_b.append(view(AR, o, [128, c.DC * 128], BF16)); o += c.DC * 128
        wv_b.append(view(AR, o, [128, c.DC * 128], BF16)); o += c.DC * 128
    QTt = view(AR, o, [128, c.S], BF16); o += c.S
    KTt = view(AR, o, [128, c.S], BF16); o += c.S
    Vtmp = view(AR, o, [128, c.S], BF16); o += c.S
    Vsb = view(AR, o, [128, c.S], BF16); o += c.S
    probsTb = []
    for par in range(2):
        probsTb.append(view(AR, o, [128, c.QB], BF16)); o += c.QB
    crow = view(AR, o, [128, c.S], BF16)[0:1, :]; o += c.S
    negmb = view(AR, o, [128, 2], BF16); o += 2
    zrow = view(AR, o, [128, 512], F32)[0:1, :]; o += 1024
    recb = view(AR, o, [128, c.QB], F32); o += 2 * c.QB
    avpan = view(AR, o, [128, 512], BF16); o += 512
    smb = view(AR, o, [128, 8], F32); o += 16
    atb = view(AR, o, [128, c.DC * c.SW], BF16); o += c.DC * c.SW
    oevac = view(AR, o, [128, c.SW], BF16); o += c.SW
    assert o <= SMB, ("attn", o, SMB)

    JTE = c.JTE
    B = []
    o = 0
    for k in range(7):
        B.append(view(AR, o, [128, JTE], F32)); o += JTE * 2
    Obf = view(AR, o, [128, JTE], BF16); o += JTE
    trow = view(AR, o, [128, 512], F32); o += 1024
    totch = view(AR, o, [128, 2 * c.E], F32); o += 4 * c.E
    basej = view(AR, o, [128, 2 * c.E], F32); o += 4 * c.E
    rt_end = o
    assert rt_end <= SMB, ("routing", rt_end, SMB)

    def v_je(ap):
        return ap.rearrange("p (j e) -> p j e", e=c.E)

    def v_ej(ap):
        return ap.rearrange("p (j e) -> p e j", e=c.E)

    o = SMB
    def smv(dtype=F32):
        nonlocal o
        r = view(AR, o, [128, c.JT], dtype); o += c.JT * 2
        return r
    m1 = smv(); m2 = smv(); idx1 = smv(); idx2 = smv(); q1 = smv(); q2 = smv()
    w1 = smv(); w2 = smv(); wn1 = smv(); wn2 = smv(); pos1 = smv(); pos2 = smv()
    val1 = smv(); val2 = smv(); mAv = smv(); pAv = smv(); mBv = smv(); pBv = smv()
    wsum = smv(); winv = smv()
    ri1 = smv(I32); ri2 = smv(I32)
    it1 = smv(I32); it2 = smv(I32); it3 = smv(I32); it4 = smv(I32)
    it5 = smv(I32); it6 = smv(I32)
    assert o <= t.ARENA

    o = 0
    Pb = []
    for k in range(4):
        Pb.append(view(AR, o, [128, c.CAP], BF16)); o += c.CAP
    sb2 = view(AR, o, [128, c.CAP], F32); o += c.CAP * 2
    tokrow = view(AR, o, [128, c.CAP], F32); o += c.CAP * 2
    tokrowi = view(AR, o, [128, c.CAP], I32); o += c.CAP * 2
    assert o <= SMB, ('compaction', o, SMB)

    o = 0
    ws1_sb = view(AR, o, [128, c.DC * c.I], BF16); o += c.DC * c.I
    ws3_sb = view(AR, o, [128, c.DC * c.I], BF16); o += c.DC * c.I
    xnTblk = [view(AR, o, [128, c.DC * c.NB], BF16)]; o += c.DC * c.NB
    xnTblk.append(xnTblk[0])
    hTblk = view(AR, o, [128, c.IC * c.NB], BF16); o += c.IC * c.NB
    silg = view(AR, o, [128, 512], BF16); o += 512
    shev = view(AR, o, [128, 512], BF16); o += 512
    ws2_sb = view(AR, o, [128, c.IC * c.D], BF16); o += c.IC * c.D
    assert o <= t.ARENA, ("shared", o, t.ARENA)

    o = 0
    bufT = view(AR, o, [128, c.DC * c.CAP], BF16)
    bufT3 = bufT.rearrange("p (dc s) -> p dc s", s=c.CAP)
    o += c.DC * c.CAP
    hTe = view(AR, o, [128, c.IC * c.CAP], BF16); o += c.IC * c.CAP
    gx = [view(AR, o, [128, c.D], BF16)]; o += c.D
    gx.append(view(AR, o, [128, c.D], BF16)); o += c.D
    w13_b = []
    for par in range(2):
        w13_b.append(view(AR, o, [128, 2 * c.DC * 128], BF16)); o += 2 * c.DC * 128
    w2_b = []
    for par in range(2):
        w2_b.append(view(AR, o, [128, c.IC * 128], BF16)); o += c.IC * 128
    epan = view(AR, o, [128, 4 * c.D], BF16); o += 4 * c.D
    eev = view(AR, o, [128, 512], BF16); o += 512
    sile = view(AR, o, [128, 512], BF16); o += 512
    assert o <= SMB, ("expert", o, SMB)

    NSET = 4
    o = 0
    g0s, g1s, xt2s, yvs = [], [], [], []
    for k in range(NSET):
        g0s.append(view(AR, o, [128, c.D], BF16)); o += c.D
        g1s.append(view(AR, o, [128, c.D], BF16)); o += c.D
        xt2s.append(view(AR, o, [128, c.D], F32)); o += 2 * c.D
        yvs.append(view(AR, o, [128, c.D], F32)); o += 2 * c.D
    assert o <= 56320 if c.D >= 2048 else o <= SMB
    oA = 56320 if c.D >= 2048 else SMB - 6 * c.D
    xt2 = view(AR, oA, [128, c.D], F32); oA += 2 * c.D
    aot = view(AR, oA, [128, c.D], BF16); oA += c.D
    sht = view(AR, oA, [128, c.D], BF16); oA += c.D
    acc = view(AR, oA, [128, c.D], F32); oA += 2 * c.D
    assert oA <= (SMB if c.D >= 2048 else t.ARENA), oA

    psA = t.psA.ap()
    psB = t.psB.ap()
    psC = t.psC.ap()
    psD = t.psD.ap()
    psE = t.psE.ap()
    cs = t.csb

    for name in t.const_names:
        p.dma("sync", cs[name], t.cdr[name].ap(), "ldc")
    p.dma("sync", t.cmeta_sb.ap(), t.cmeta.ap(), "ldc")
    mC = p.mark("ldc")
    for e in ENGS:
        p.wait(e, "ldc", mC)

    tile_rel = {}
    panel_flush = {0: ("stp0", 0), 1: ("stp0", 0)}
    xnb_st = {0: 0, 1: 0}
    xnf_rel = {0: 0, 1: 0}
    xnf_m = {}

    def p0_stats_stages(i):
        bi = i % 2
        lxs = "lx0" if bi == 0 else "lx1"
        q0 = bi * 8
        sst = {}

        def s0():
            relv, rels = tile_rel.get(i - 2, (0, 0))
            w_ld = p.dma("sync", xtile[bi],
                         t.x_loc.ap()[i * 128:(i + 1) * 128, :],
                         lxs, wait=[("evv", relv), ("evs", rels)])
            p.wait("vector", lxs, w_ld)
            p.wait("scalar", lxs, w_ld)
            p.vg("reduce_sum", st_[:, q0:q0 + 1], xtile[bi], AX.X)
            sst['mss'] = p.s("activation", sq, xtile[bi], AF.Square, sem="evs")

        def s1():
            p.wait("vector", "evs", sst['mss'])
            p.vg("reduce_sum", st_[:, q0 + 1:q0 + 2], sq, AX.X)

        def s2():
            p.vg("tensor_scalar_mul", st_[:, q0 + 2:q0 + 3], st_[:, q0:q0 + 1],
                 1.0 / c.D)
            p.vg("tensor_mul", st_[:, q0 + 3:q0 + 4], st_[:, q0 + 2:q0 + 3],
                 st_[:, q0 + 2:q0 + 3])

        def s3():
            mvar = p.vg("scalar_tensor_tensor", st_[:, q0 + 4:q0 + 5],
                        st_[:, q0 + 1:q0 + 2], 1.0 / c.D,
                        st_[:, q0 + 3:q0 + 4],
                        op0=ALU.mult, op1=ALU.subtract)
            sst['msd'] = p.s("activation", st_[:, q0 + 5:q0 + 6],
                             st_[:, q0 + 4:q0 + 5],
                             AF.Sqrt, bias=1e-5, sem="evs",
                             wait=[("evv", mvar)])

        def s4():
            p.vg("reciprocal", st_[:, q0 + 6:q0 + 7], st_[:, q0 + 5:q0 + 6],
                 wait=[("evs", sst['msd'])])

        def s5():
            mxnf = p.v("tensor_scalar", xnf[bi], xtile[bi],
                       st_[:, q0 + 2:q0 + 3], st_[:, q0 + 6:q0 + 7],
                       op0=ALU.subtract, op1=ALU.mult, sem="evv",
                       wait=[("mm", xnf_rel[bi])])
            mxnb = p.v("tensor_scalar", xnb[bi], xtile[bi],
                       st_[:, q0 + 2:q0 + 3], st_[:, q0 + 6:q0 + 7],
                       op0=ALU.subtract, op1=ALU.mult, sem="evv",
                       wait=[("sxn" if bi == 0 else "sxn1", xnb_st[bi])])
            xnf_m[i] = mxnf
            tile_rel[i] = (mxnb, sst['mss'])
            xnb_st[bi] = p.dma("sync", t.xnin.ap()[i * 128:(i + 1) * 128, :],
                               xnb[bi], "sxn" if bi == 0 else "sxn1",
                               wait=[("evv", mxnb)])
            if t.dbgst is not None:
                p.dma("sync", t.dbgst.ap()[:, i * 16:(i + 1) * 16], st_, "sta",
                      wait=[("evv", mxnb)])

        return [s0, s1, s2, s3, s4, s5]

    for f in p0_stats_stages(0):
        f()
    stage_at = {}
    for k_ in range(6):
        stage_at.setdefault(min(c.DC - 1, (k_ * c.DC) // 6), []).append(k_)
    for i in range(c.NT):
        bi = i % 2
        stages = p0_stats_stages(i + 1) if i + 1 < c.NT else None
        mxnf = xnf_m[i]
        gsel = (i // c.PGRP) % 2
        W4 = min(4, c.DC)
        ngr = c.DC // W4
        early5 = ngr >= 4
        gmap = [0, 0, 1, 1, 2] if early5 else [0, 0, 1, 1, 2, 3]
        gr_stage = {}
        for k_ in range(5 if early5 else 6):
            gr_stage.setdefault(min(ngr - 1, (gmap[k_] * ngr) // 4),
                                []).append(k_)
        pT3 = panelT[gsel].rearrange("p (dc w) -> p dc w", w=128 * c.PGRP)
        off = (i % c.PGRP) * 128
        for g in range(ngr):
            pb = g % 2
            pst = psC if pb == 0 else psD
            bkey = "C" if pb == 0 else "D"
            if stages is not None and early5 and g == ngr - 1:
                stages[5]()
            for j in range(W4):
                dc = g * W4 + j
                p.tr(pst[:, j * 128:(j + 1) * 128],
                     xnf[bi][:, dc * 128:(dc + 1) * 128], cs["identf"],
                     sem="mm" if j == W4 - 1 else None,
                     wait=[("evv", mxnf), p.bkw(bkey)] if j == 0 else None)
            mtr = p.mark("mm")
            p.wait("vector", "mm", mtr)
            p.v("tensor_copy", xnch[pb][:, 0:W4 * 128], pst[:, 0:W4 * 128],
                wait=[("mm", p.scratch.get(("lgrel", pb), 0))])
            ps3 = pst[:, 0:W4 * 128].rearrange("p (dc w) -> p dc w", w=128)
            mev = p.v("tensor_copy",
                      pT3[:, g * W4:(g + 1) * W4, off:off + 128],
                      ps3, sem="evv",
                      wait=[panel_flush[gsel]]
                      if (g == 0 and i % c.PGRP == 0) else None)
            p.bks(bkey, "evv", mev)
            for j in range(W4):
                dc = g * W4 + j
                p.mm(psB[:, 0:c.E], xnch[pb][:, j * 128:(j + 1) * 128],
                     cs["gwT"][:, dc * c.E:(dc + 1) * c.E],
                     start=(dc == 0), stop=(dc == c.DC - 1),
                     sem="mm" if j == W4 - 1 else None,
                     wait=[("evv", mev), p.bkw("B")] if dc == 0 else
                          ([("evv", mev)] if j == 0 else None))
            p.scratch[("lgrel", pb)] = p.mark("mm")
            if stages is not None and g in gr_stage:
                for k_ in gr_stage[g]:
                    stages[k_]()
        xnf_rel[bi] = p.mark("mm")
        mlg = p.v("tensor_copy", logit_sb[:, i * c.E:(i + 1) * c.E], psB[:, 0:c.E],
                  sem="evv", wait=[("mm", xnf_rel[bi])])
        p.bks("B", "evv", mlg)
        if (i + 1) % c.PGRP == 0:
            i4 = i // c.PGRP
            mflush = p.mark("evv")
            bsem = "stp0" if i4 % 2 == 0 else "stp1"
            for dc in range(c.DC):
                p.dma(
                    "scalar",
                    t.xnT_d.ap()[dc * 128:(dc + 1) * 128,
                                 i4 * 128 * c.PGRP:(i4 + 1) * 128 * c.PGRP],
                    panelT[i4 % 2][:, dc * 128 * c.PGRP:(dc + 1) * 128 * c.PGRP],
                    bsem, wait=[("evv", mflush)] if dc == 0 else None)
            panel_flush[i4 % 2] = (bsem, p.mark(bsem))
            p.scratch.setdefault("stpb", {})[(i + 1) * 128] = panel_flush[i4 % 2]
    mlgall = p.mark("evv")
    p.dma("sync", t.lgin.ap(), logit_sb, "sro", wait=[("evv", mlgall)])
    mlgst = p.mark("sro")
    mstp0 = p.mark("stp0")
    mstp1 = p.mark("stp1")
    mstp_b = p.scratch.get("stpb", {})
    msxn = p.mark("sxn")
    msxn1 = p.mark("sxn1")
    p.g("collective_compute", "AllGather", ALU.bypass,
        replica_groups=[list(range(c.NC))],
        ins=[t.lgin.ap().opt()], outs=[t.lgout.ap().opt()],
        sem="cc1", n=1, wait=[("sro", mlgst), ("sxn", msxn), ("sxn1", msxn1)])
    p.g("collective_compute", "AllGather", ALU.bypass,
        replica_groups=[list(range(c.NC))],
        ins=[t.xnin.ap().opt()], outs=[t.xnout.ap().opt()],
        sem="cc2", n=1)

    mWfree = {}
    gqt = 0
    for b in range(c.BL):
        mo_done = p.scratch.get("outproj_mm", 0)
        for dc in range(c.DC):
            p.dma("sync", xnTb[:, dc * c.S:(dc + 1) * c.S],
                  t.xnT_d.ap()[dc * 128:(dc + 1) * 128, b * c.S:(b + 1) * c.S],
                  "ldx",
                  wait=[("stp0", mstp0), ("stp1", mstp1), ("sxn", msxn),
                        ("sxn1", msxn1),
                        ("sro", mlgst), ("mm", mo_done)] if dc == 0 else None)
        mxnT = p.mark("ldx")
        for h in range(c.H):
            par = h % 2
            lws = "lwa" if par == 0 else "lwb"
            wfree = mWfree.get((b, h - 2), 0)
            for wi, wsb in enumerate((wq_b[par], wk_b[par], wv_b[par])):
                w = None
                if wi == 0:
                    w = [("mm", wfree)]
                    if b == 0 and h < 2:
                        w += [("stp0", mstp0), ("stp1", mstp1),
                              ("sxn", msxn), ("sxn1", msxn1),
                              ("sro", mlgst)]
                p.dma("gpsimd", wsb, t.wqkvp.ap()[h, wi], lws, wait=w)
            mw = p.mark(lws)
            nblkS = c.S // c.SW

            def qkv(dst, wsb, eng):
                last = None
                for nb in range(nblkS):
                    ps = psC if nb % 2 == 0 else psD
                    bkey = "C" if nb % 2 == 0 else "D"
                    for dc in range(c.DC):
                        w = None
                        if dc == 0:
                            w = [(lws, mw), ("ldx", mxnT), p.bkw(bkey)]
                        p.mm(ps[:, 0:c.SW], wsb[:, dc * 128:(dc + 1) * 128],
                             xnTb[:, dc * c.S + nb * c.SW:dc * c.S + (nb + 1) * c.SW],
                             start=(dc == 0), stop=(dc == c.DC - 1),
                             sem="mm" if dc == c.DC - 1 else None, wait=w)
                    mgrp = p.mark("mm")
                    if eng == "vector":
                        m = p.v("tensor_copy", dst[:, nb * c.SW:(nb + 1) * c.SW],
                                ps[:, 0:c.SW], sem="evv", wait=[("mm", mgrp)])
                        last = ("evv", m)
                    else:
                        m = p.s("activation", dst[:, nb * c.SW:(nb + 1) * c.SW],
                                ps[:, 0:c.SW], AF.Copy, sem="evs",
                                wait=[("mm", mgrp)])
                        last = ("evs", m)
                    p.bks(bkey, *last)
                return last
            sQ, mQ = qkv(QTt, wq_b[par], "vector")
            sK, mK = qkv(KTt, wk_b[par], "scalar")
            sV, mV = qkv(Vtmp, wv_b[par], "vector")
            mWfree[(b, h)] = p.mark("mm")
            mVev = 0
            for grp in range(0, c.SC, 8):
                n8 = min(8, c.SC - grp)
                for k8 in range(n8):
                    scix = grp + k8
                    p.tr(psE[:, k8 * 128:(k8 + 1) * 128],
                         Vtmp[:, scix * 128:(scix + 1) * 128], cs["identb"],
                         sem="mm" if k8 == n8 - 1 else None,
                         wait=[(sV, mV), p.bkw("E")] if k8 == 0 else None)
                mtr8 = p.mark("mm")
                mVev = p.v("tensor_copy", Vsb[:, grp * 128:(grp + n8) * 128],
                           psE[:, 0:n8 * 128], sem="evv", wait=[("mm", mtr8)])
                p.bks("E", "evv", mVev)
            def p1_tail(qt):
                mc = p.vg("tensor_copy", negmb[:, 0:1], smb[:, 4:5])
                mtc = p.tr(psE[0:1, 0:128], negmb[:, 0:1], cs["identb"],
                           sem="mm", wait=[("evv", mc), p.bkw("E")])
                crel = p.v("tensor_copy", crow[0:1, qt * 128:(qt + 1) * 128],
                           psE[0:1, 0:128], sem="evv", wait=[("mm", mtc)])
                p.bks("E", "evv", crel)
                p.scratch["crel"] = crel
            for qt in range(c.SC):
                ext_t = qt + 1
                ext = ext_t * 128
                nch = (ext + 511) // 512
                chmms = []
                for nb2 in range(nch):
                    lo = nb2 * 512
                    hi = min(ext, lo + 512)
                    w = [p.bkw("A%d" % (lo // 512))]
                    if nb2 == 0:
                        w += [("evs", mK), ("evv", mQ)]
                    p.mm(psA[:, lo:hi], QTt[:, qt * 128:(qt + 1) * 128],
                         KTt[:, lo:hi], start=True, stop=True,
                         sem="mm", wait=w)
                    chmms.append(p.mark("mm"))
                if qt > 0:
                    p1_tail(qt - 1)
                p.vg("tensor_add", psA[:, ext - 128:ext], psA[:, ext - 128:ext],
                     cs["maskd"], wait=[("mm", chmms[-1])])
                for nb2 in range(nch):
                    lo = nb2 * 512
                    hi = min(ext, lo + 512)
                    mmx = p.vg("tensor_reduce", smb[:, nb2:nb2 + 1],
                               psA[:, lo:hi], AX.X, ALU.max, negate=True,
                               wait=[("mm", chmms[nb2])] if nb2 < nch - 1 else None)
                    p.bks("A%d" % nb2, "evv", mmx)
                p.vg("tensor_reduce", smb[:, 4:5], smb[:, 0:nch], AX.X, ALU.min)
                if qt == c.SC - 1:
                    p1_tail(qt)
            mcrow = p.scratch.get("crel", 0)
            for qq in range(c.S // c.QB):
                pav = psC if qq % 2 == 0 else psD
                avkey = "C" if qq % 2 == 0 else "D"
                relav = p.bkw(avkey)
                relB = p.bkw("B")
                nkc = (qq + 1) * c.KCB
                pend = {}

                def p2_av(kc):
                    ppar = kc % 2
                    wsem, mexp = pend.pop(kc)
                    p.mm(pav[:, 0:c.QB], Vsb[:, kc * 128:(kc + 1) * 128],
                         probsTb[ppar], start=(kc == 0), stop=(kc == nkc - 1),
                         sem="mm",
                         wait=[(wsem, mexp), ("evv", mVev), relav]
                         if kc == 0 else [(wsem, mexp)])
                    p.mm(psB[0:1, 0:c.QB], cs["onescol_b"], probsTb[ppar],
                         start=(kc == 0), stop=(kc == nkc - 1),
                         sem="mm" if kc == nkc - 1 else None,
                         wait=[relB] if kc == 0 else None)
                    p.scratch[("ptb", ppar)] = p.mark("mm")

                for kc in range(nkc):
                    bA = kc % 4
                    pst = psA[:, bA * 512:bA * 512 + c.QB]
                    p.mm(pst, KTt[:, kc * 128:(kc + 1) * 128],
                         QTt[:, qq * c.QB:(qq + 1) * c.QB],
                         start=True, stop=False,
                         wait=[p.bkw("A%d" % bA), ("evv", mcrow)])
                    p.mm(pst, cs["onesrow_b"],
                         crow[0:1, qq * c.QB:(qq + 1) * c.QB],
                         start=False, stop=True, sem="mm")
                    mscT = p.mark("mm")
                    ppar = kc % 2
                    prel = p.scratch.get(("ptb", ppar), 0)
                    d = kc - qq * c.KCB
                    mexp = p.s("activation", probsTb[ppar], pst, AF.Exp,
                               scale=sc_, sem="evs",
                               wait=[("mm", max(mscT, prel))])
                    p.bks("A%d" % bA, "evs", mexp)
                    if d >= 0:
                        mexp = p.v("tensor_tensor", probsTb[ppar],
                                   probsTb[ppar],
                                   cs["maskT4b"][:, d * c.QB:(d + 1) * c.QB],
                                   ALU.mult, sem="evv", wait=[("evs", mexp)])
                        wsem = "evv"
                    else:
                        wsem = "evs"
                    pend[kc] = (wsem, mexp)
                    if kc > 0:
                        p2_av(kc - 1)
                p2_av(nkc - 1)
                mavT = p.mark("mm")
                mze = p.vg("tensor_copy", zrow[0:1, 0:c.QB], psB[0:1, 0:c.QB],
                           wait=[("mm", mavT)])
                mzr = p.vg("reciprocal", zrow[0:1, 0:c.QB], zrow[0:1, 0:c.QB])
                mbc = p.mm(psB[:, 0:c.QB], cs["onesrow"], zrow[0:1, 0:c.QB],
                           start=True, stop=True, sem="mm",
                           wait=[("evv", mzr)])
                mrb = p.v("tensor_copy", recb, psB[:, 0:c.QB], sem="evv",
                          wait=[("mm", mbc)])
                p.bks("B", "evv", mrb)
                mavf = p.scratch.get("avflush", 0)
                mavev = p.v("scalar_tensor_tensor", avpan[:, 0:c.QB],
                            pav[:, 0:c.QB], 1.0, recb, op0=ALU.mult,
                            op1=ALU.mult, sem="evv",
                            wait=[("evv", mrb), ("sta", mavf)])
                p.bks(avkey, "evv", mavev)
                mfl = p.dma(
                    "sync",
                    t.attnT_d.ap()[h * 128:(h + 1) * 128,
                                   b * c.S + qq * c.QB:b * c.S + (qq + 1) * c.QB],
                    avpan[:, 0:c.QB], "sta", wait=[("evv", mavev)])
                p.scratch["avflush"] = mfl
        msta = p.mark("sta")
        for dc in range(c.DC):
            p.dma("gpsimd", woT_sb[:, dc * c.D:(dc + 1) * c.D],
                  t.woT.ap()[dc * 128:(dc + 1) * 128, :], "lwc",
                  wait=[("mm", mWfree[(b, c.H - 1)]),
                        ("evs", p.scratch.get("psArel", 0))] if dc == 0 else None)
        mwo = p.mark("lwc")
        for blk in range(c.S // c.SW):
            atbrel = p.scratch.get("atbrel", 0)
            for dc in range(c.DC):
                p.dma("sync", atb[:, dc * c.SW:(dc + 1) * c.SW],
                      t.attnT_d.ap()[dc * 128:(dc + 1) * 128,
                                     b * c.S + blk * c.SW:b * c.S + (blk + 1) * c.SW],
                      "ldx",
                      wait=[("sta", msta), ("mm", atbrel)] if dc == 0 else None)
            matb = p.mark("ldx")
            for fc in range(c.DC):
                ps = psC if fc % 2 == 0 else psD
                bkey = "C" if fc % 2 == 0 else "D"
                for dc in range(c.DC):
                    p.mm(ps[:, 0:c.SW],
                         woT_sb[:, dc * c.D + fc * 128:dc * c.D + (fc + 1) * 128],
                         atb[:, dc * c.SW:(dc + 1) * c.SW],
                         start=(dc == 0), stop=(dc == c.DC - 1),
                         sem="mm" if dc == c.DC - 1 else None,
                         wait=[("ldx", matb), ("lwc", mwo), p.bkw(bkey)]
                         if dc == 0 else None)
                mog = p.mark("mm")
                moev = p.v("tensor_copy", oevac, ps[:, 0:c.SW], sem="evv",
                           wait=[("mm", mog), ("sao", p.scratch.get("oevst", 0))])
                p.bks(bkey, "evv", moev)
                mst = p.dma(
                    "sync",
                    t.aoutdT.ap()[fc * 128:(fc + 1) * 128,
                                  b * c.S + blk * c.SW:b * c.S + (blk + 1) * c.SW],
                    oevac, "sao", wait=[("evv", moev)])
                p.scratch["oevst"] = mst
            p.scratch["atbrel"] = p.mark("mm")
        p.scratch["outproj_mm"] = p.mark("mm")
    mattn = p.mark("mm")

    mcc1 = p.mark("cc1")
    msao_a = p.mark("sao")
    nper = c.NT * c.E
    for r in range(c.NC):
        p.dma("sync", B[0][:, r * nper:(r + 1) * nper],
              t.lgout.ap()[r * 128:(r + 1) * 128, :], "ldx",
              wait=[("cc1", mcc1), ("mm", mattn), ("sao", msao_a)]
              if r == 0 else None)
    mlgld = p.mark("ldx")
    p.wait("vector", "ldx", mlgld)
    p.wait("scalar", "ldx", mlgld)
    p.wait("vector", "mm", mattn)
    p.wait("scalar", "mm", mattn)
    p.wait("vector", "sao", msao_a)
    p.wait("scalar", "sao", msao_a)
    p.sg("activation", B[1], B[0], AF.Abs)
    p.sg("activation", B[2], B[1], AF.Exp, scale=-1.0)
    mln1 = p.sg("activation", B[1], B[2], AF.Ln, bias=1.0)
    msp = p.v("scalar_tensor_tensor", B[2], B[0], 0.0, B[1], op0=ALU.max,
              op1=ALU.add, sem="evv", wait=[("evs", mln1)])
    msco = p.s("activation", B[3], B[2], AF.Sqrt, sem="evs", wait=[("evv", msp)])
    p.wait("vector", "evs", msco)
    gbb = cs["gateb"].unsqueeze(1).broadcast_to([128, c.JT, c.E])
    p.vg("tensor_tensor", v_je(B[0]), v_je(B[3]), gbb, ALU.add)
    p.vg("tensor_reduce", m1, v_je(B[0]), AX.X, ALU.max)
    m1b = m1.unsqueeze(1).broadcast_to([128, c.E, c.JT])
    p.vg("tensor_tensor", v_ej(B[1]), v_ej(B[0]), m1b, ALU.is_equal)
    ierb = cs["iotaE2"][:, c.E:2 * c.E].unsqueeze(1).broadcast_to(
        [128, c.JT, c.E])
    p.vg("scalar_tensor_tensor", v_je(B[1]), v_je(B[1]), 1.0e6, ierb,
        op0=ALU.mult, op1=ALU.add)
    p.vg("tensor_reduce", q1, v_je(B[1]), AX.X, ALU.max)
    p.vg("tensor_scalar", idx1, q1, -1.0, float(1.0e6 + c.E - 1),
         op0=ALU.mult, op1=ALU.add)
    q1b = q1.unsqueeze(1).broadcast_to([128, c.E, c.JT])
    p.vg("tensor_tensor", v_ej(B[4]), v_ej(B[1]), q1b, ALU.is_equal)
    p.vg("scalar_tensor_tensor", B[1], B[4], NEG, B[0], op0=ALU.mult, op1=ALU.add)
    p.vg("tensor_reduce", m2, v_je(B[1]), AX.X, ALU.max)
    m2b = m2.unsqueeze(1).broadcast_to([128, c.E, c.JT])
    p.vg("tensor_tensor", v_ej(B[2]), v_ej(B[1]), m2b, ALU.is_equal)
    p.vg("scalar_tensor_tensor", v_je(B[2]), v_je(B[2]), 1.0e6, ierb,
        op0=ALU.mult, op1=ALU.add)
    p.vg("tensor_reduce", q2, v_je(B[2]), AX.X, ALU.max)
    p.vg("tensor_scalar", idx2, q2, -1.0, float(1.0e6 + c.E - 1),
         op0=ALU.mult, op1=ALU.add)
    q2b = q2.unsqueeze(1).broadcast_to([128, c.E, c.JT])
    p.vg("tensor_tensor", v_ej(B[5]), v_ej(B[2]), q2b, ALU.is_equal)
    p.vg("tensor_add", B[6], B[4], B[5])
    mO = p.vg("tensor_copy", Obf, B[6])
    p.vg("tensor_mul", B[1], B[4], B[3])
    p.vg("tensor_reduce", w1, v_je(B[1]), AX.X, ALU.add)
    p.vg("tensor_mul", B[1], B[5], B[3])
    p.vg("tensor_reduce", w2, v_je(B[1]), AX.X, ALU.add)
    p.vg("tensor_add", wsum, w1, w2)
    p.vg("reciprocal", winv, wsum)
    p.vg("tensor_mul", wn1, w1, winv)
    p.vg("tensor_mul", wn2, w2, winv)
    nbat = 512 // c.E
    relBt = 0
    for jb in range(0, c.JT, nbat):
        nthis = min(nbat, c.JT - jb)
        ps = psC if (jb // nbat) % 2 == 0 else psD
        bkey = "C" if (jb // nbat) % 2 == 0 else "D"
        for k in range(nthis):
            j = jb + k
            p.mm(ps[:, k * c.E:(k + 1) * c.E], cs["tmat"],
                 Obf[:, j * c.E:(j + 1) * c.E], start=True, stop=True,
                 wait=[("evv", mO), p.bkw(bkey), p.bkw("B")] if k == 0 else None)
            p.mm(psB[0:1, k * c.E:(k + 1) * c.E], cs["onescol_b"],
                 Obf[:, j * c.E:(j + 1) * c.E], start=True, stop=True,
                 sem="mm" if k == nthis - 1 else None)
        mgrp = p.mark("mm")
        msie = p.v("tensor_copy", B[2][:, jb * c.E:(jb + nthis) * c.E],
                   ps[:, 0:nthis * c.E], sem="evv", wait=[("mm", mgrp)])
        p.bks(bkey, "evv", msie)
        relBt = p.v("tensor_copy", trow[0:1, 0:nthis * c.E],
                    psB[0:1, 0:nthis * c.E], sem="evv",
                    wait=[("sro", p.scratch.get("trowst", 0))])
        p.bks("B", "evv", relBt)
        p.scratch["trowst"] = p.dma(
            "sync", t.totd.ap()[jb * c.E:(jb + nthis) * c.E].unsqueeze(0),
            trow[0:1, 0:nthis * c.E], "sro", wait=[("evv", relBt)])
    mtot = p.mark("sro")
    nchk = (c.JT + 127) // 128
    assert nchk <= 2
    mtz = p.v("memset", totch, 0.0, sem="evv")
    for ch in range(nchk):
        rows = min(128, c.JT - ch * 128)
        p.dma("sync", totch[0:rows, ch * c.E:(ch + 1) * c.E],
              t.totd.ap()[ch * 128 * c.E:(ch * 128 + rows) * c.E]
              .rearrange("(r e) -> r e", e=c.E), "ldx",
              wait=[("sro", mtot), ("evv", mtz)] if ch == 0 else None)
    mtch = p.mark("ldx")
    for ch in range(nchk):
        lastmm = (ch == nchk - 1) and (nchk == 1)
        p.mm(psB[:, ch * c.E:(ch + 1) * c.E], cs["tmatf"][:, 0:128],
             totch[:, ch * c.E:(ch + 1) * c.E],
             start=True, stop=(ch == 0 or nchk == 1),
             sem="mm" if lastmm else None,
             wait=[("ldx", mtch), p.bkw("B")] if ch == 0 else None)
        if ch == 1:
            p.mm(psB[:, c.E:2 * c.E], cs["tmatf"][:, 128:256],
                 totch[:, 0:c.E], start=False, stop=True, sem="mm")
    mbmm = p.mark("mm")
    mbev = p.v("tensor_copy", basej[:, 0:nchk * c.E], psB[:, 0:nchk * c.E],
               sem="evv", wait=[("mm", mbmm)])
    p.bks("B", "evv", mbev)
    for ch in range(nchk):
        rows = min(128, c.JT - ch * 128)
        p.dma("sync", t.based.ap()[ch * 128 * c.E:(ch * 128 + rows) * c.E]
              .rearrange("(r e) -> r e", e=c.E),
              basej[0:rows, ch * c.E:(ch + 1) * c.E], "sro",
              wait=[("evv", mbev)])
    mbst = p.mark("sro")
    if JTE <= 512:
        baserow = trow[0:1, 0:JTE]
    else:
        baserow = B[1][0:1, 0:JTE]
    mrow = p.dma("sync", baserow, t.based.ap().unsqueeze(0), "ldx",
                 wait=[("sro", mbst)])
    nbc = (JTE + 511) // 512
    for nb in range(nbc):
        lo, hi = nb * 512, min(JTE, (nb + 1) * 512)
        ps = psC if nb % 2 == 0 else psD
        bkey = "C" if nb % 2 == 0 else "D"
        p.mm(ps[:, 0:hi - lo], cs["onesrow"], baserow[0:1, lo:hi],
             start=True, stop=True, sem="mm",
             wait=[("ldx", mrow), p.bkw(bkey)])
        mg = p.mark("mm")
        p.vg("tensor_sub", B[2][:, lo:hi], B[2][:, lo:hi], B[6][:, lo:hi],
             wait=[("mm", mg)])
        mz = p.vg("tensor_add", B[2][:, lo:hi], B[2][:, lo:hi],
                  ps[:, 0:hi - lo])
        p.bks(bkey, "evv", mz)
    p.vg("tensor_mul", B[1], B[4], B[2])
    p.vg("tensor_reduce", pos1, v_je(B[1]), AX.X, ALU.add)
    p.vg("tensor_mul", B[1], B[5], B[2])
    p.vg("tensor_reduce", pos2, v_je(B[1]), AX.X, ALU.add)
    selAb = cs["selAB"][:, 0:c.E].unsqueeze(1).broadcast_to([128, c.JT, c.E])
    selBb = cs["selAB"][:, c.E:2 * c.E].unsqueeze(1).broadcast_to(
        [128, c.JT, c.E])
    p.vg("tensor_tensor", v_je(B[1]), v_je(B[6]), selAb, ALU.mult)
    p.vg("tensor_reduce", mAv, v_je(B[1]), AX.X, ALU.add)
    p.vg("tensor_tensor", v_je(B[1]), v_je(B[2]), selAb, ALU.mult)
    p.vg("tensor_reduce", pAv, v_je(B[1]), AX.X, ALU.add)
    p.vg("tensor_tensor", v_je(B[1]), v_je(B[6]), selBb, ALU.mult)
    p.vg("tensor_reduce", mBv, v_je(B[1]), AX.X, ALU.add)
    p.vg("tensor_tensor", v_je(B[1]), v_je(B[2]), selBb, ALU.mult)
    p.vg("tensor_reduce", pBv, v_je(B[1]), AX.X, ALU.add)
    p.vg("tensor_scalar", val1, pos1, float(c.CAP), None, op0=ALU.is_lt)
    p.vg("tensor_scalar", val2, pos2, float(c.CAP), None, op0=ALU.is_lt)
    p.vg("tensor_mul", wn1, wn1, val1)
    p.vg("tensor_mul", wn2, wn2, val2)
    for (posx, idxx, valx, rix) in ((pos1, idx1, val1, ri1),
                                    (pos2, idx2, val2, ri2)):
        p.vg("tensor_copy", it1, posx)
        p.vg("tensor_scalar", it2, it1, c.A - 1, None, op0=ALU.bitwise_and)
        p.vg("tensor_scalar", it2, it2, 7, None, op0=ALU.logical_shift_left)
        p.vg("tensor_scalar", it3, it1, c.LA, None,
             op0=ALU.logical_shift_right)
        p.vg("tensor_add", it2, it2, it3)
        p.vg("tensor_scalar", it3, it2, c.LCW, None,
             op0=ALU.logical_shift_right)
        p.vg("tensor_scalar", it5, it2, c.CW - 1, None,
             op0=ALU.bitwise_and)
        p.vg("tensor_copy", it1, idxx)
        p.vg("tensor_scalar", it6, it1, c.LNC, None,
             op0=ALU.logical_shift_right)
        p.vg("tensor_scalar", it1, it1, c.NC - 1, None,
             op0=ALU.bitwise_and)
        p.vg("tensor_scalar", it6, it6, c.LNCB, None,
             op0=ALU.logical_shift_left)
        p.vg("tensor_add", it6, it6, it3)
        p.vg("tensor_scalar", it6, it6, c.LNC, None,
             op0=ALU.logical_shift_left)
        p.vg("tensor_add", it6, it6, it1)
        p.vg("tensor_scalar", it6, it6, c.LCW, None,
             op0=ALU.logical_shift_left)
        p.vg("tensor_add", it2, it6, it5)
        p.vg("tensor_copy", it4, valx)
        p.vg("tensor_tensor", rix, it2, it4, ALU.mult)
    if t.dbgrt is not None:
        dlist = [m1, m2, idx1, idx2, w1, w2, wn1, wn2, pos1, pos2, val1,
                 val2, mAv, pAv, mBv, pBv, wsum, winv]
        mduw = p.v("tensor_copy", it1, it1, sem="evv")
        for kk, sv in enumerate(dlist):
            p.dma("sync", t.dbgrt.ap()[:, kk * c.JT:(kk + 1) * c.JT], sv,
                  "sta", wait=[("evv", mduw)] if kk == 0 else None)
        p.dma("sync", t.dbgsco.ap(), B[3], "sta")
        p.dma("sync", t.dbgsg.ap(), B[2], "sta")
    mwv = p.vg("tensor_copy", it4, it3)
    p.dma("sync", t.r1d.ap(), ri1, "sro", wait=[("evv", mwv)])
    p.dma("sync", t.r2d.ap(), ri2, "sro")
    p.dma("sync", t.wv1d.ap(), wn1, "sro")
    mrw = p.dma("sync", t.wv2d.ap(), wn2, "sro")
    if p.cur == "gpsimd":
        coff = nc.values_load(t.cmeta_sb.ap()[0:1, 0:1],
                              engines=[mybir.EngineType.Pool],
                              min_val=0, max_val=c.NC - 1,
                              skip_runtime_bounds_check=True)
    else:
        coff = 0
    p.dma("gpsimd", t.r1l.ap(), t.r1d.ap()[:, bass.ts(coff, c.NT)], "gat",
          wait=[("sro", mrw)])
    p.dma("gpsimd", t.r2l.ap(), t.r2d.ap()[:, bass.ts(coff, c.NT)], "gat")
    p.dma("gpsimd", t.wv1l.ap(), t.wv1d.ap()[:, bass.ts(coff, c.NT)], "gat")
    mloc = p.dma("gpsimd", t.wv2l.ap(), t.wv2d.ap()[:, bass.ts(coff, c.NT)],
                 "gat")
    mroute_dve = p.mark("evv")

    nhalf = (c.CAP + 511) // 512
    pbrel = {}
    for j in range(c.JT):
        for e2 in range(2):
            pb = (2 * j + e2) % 4
            mX, pX = (mAv, pAv) if e2 == 0 else (mBv, pBv)
            mpb = p.v("tensor_scalar", Pb[pb], cs["iotaCAP"], pX[:, j:j + 1],
                      mX[:, j:j + 1], op0=ALU.is_equal, op1=ALU.mult, sem="evv",
                      wait=[("mm", pbrel.get(pb, 0))])
            for hf in range(nhalf):
                lo = hf * 512
                hi = min(c.CAP, lo + 512)
                w = [("evv", mpb)] if hf == 0 else []
                if j == 0:
                    w = list(w) + [p.bkw("A%d" % qd) for qd in range(4)]
                p.mm(psA[0:2, e2 * c.EOFF + lo:e2 * c.EOFF + hi],
                     cs["cmpT"][:, 2 * j:2 * j + 2], Pb[pb][:, lo:hi],
                     start=(j == 0), stop=(j == c.JT - 1),
                     sem="mm", wait=w)
            pbrel[pb] = p.mark("mm")
    mcmp = p.mark("mm")
    for e2 in range(2):
        mev = p.v("tensor_copy", sb2[0:2, 0:c.CAP],
                  psA[0:2, e2 * c.EOFF:e2 * c.EOFF + c.CAP], sem="evv",
                  wait=[("mm", mcmp), ("sto", p.scratch.get("tokst", 0))])
        for qd in range(4):
            p.bks("A%d" % qd, "evv", mev)
        for hf in range(nhalf):
            lo = hf * 512
            hi = min(c.CAP, lo + 512)
            ps = psB if hf == 0 else psC
            bkey = "B" if hf == 0 else "C"
            p.mm(ps[0:1, 0:hi - lo], cs["coef2"], sb2[0:2, lo:hi],
                 start=True, stop=True, sem="mm",
                 wait=[("evv", mev), p.bkw(bkey)])
            mg = p.mark("mm")
            mtr_ = p.vg("tensor_copy", tokrow[0:1, lo:hi], ps[0:1, 0:hi - lo],
                        wait=[("mm", mg)])
            p.bks(bkey, "evv", mtr_)
        mtk = p.vg("tensor_copy", tokrowi[0:1, 0:c.CAP],
                   tokrow[0:1, 0:c.CAP])
        mts = p.dma("sync", t.tokd.ap()[e2, :].unsqueeze(0),
                    tokrowi[0:1, 0:c.CAP], "sto", wait=[("evv", mtk)])
        p.scratch["tokst"] = mts
        p.dma("gpsimd", (t.tokA_sb if e2 == 0 else t.tokB_sb).ap(),
              t.tokd.ap()[e2, :].rearrange("(q a) -> q a", a=c.A), "gtk",
              wait=[("sto", mts)])
    mtok = p.mark("gtk")
    mtok_sta = p.mark("sto")

    for dc in range(c.DC):
        p.dma("gpsimd", ws1_sb[:, dc * c.I:(dc + 1) * c.I],
              t.ws1T.ap()[dc * 128:(dc + 1) * 128, :], "lwc",
              wait=[("evv", mroute_dve), ("sto", mtok_sta),
                    ("mm", mcmp)] if dc == 0 else None)
    for dc in range(c.DC):
        p.dma("gpsimd", ws3_sb[:, dc * c.I:(dc + 1) * c.I],
              t.ws3T.ap()[dc * 128:(dc + 1) * 128, :], "lwc")
    for ic in range(c.IC):
        p.dma("gpsimd", ws2_sb[:, ic * c.D:(ic + 1) * c.D],
              t.ws2T.ap()[ic * 128:(ic + 1) * 128, :], "lwc")
    mws = p.mark("lwc")
    mws2 = mws
    for blk in range(c.NBLK):
        xb = xnTblk[blk % 2]
        prev_use = p.scratch.get("xnblk1", 0)
        for dc in range(c.DC):
            p.dma("sync", xb[:, dc * c.NB:(dc + 1) * c.NB],
                  t.xnT_d.ap()[dc * 128:(dc + 1) * 128,
                               blk * c.NB:(blk + 1) * c.NB], "ldx",
                  wait=[("mm", max(prev_use, mcmp)), ("evv", mroute_dve),
                        ("stp0", mstp0), ("stp1", mstp1), ("sto", mtok_sta)]
                  if dc == 0 else None)
        mxb = p.mark("ldx")
        nsb = max(c.NB // 512, 1)
        wsb_ = min(512, c.NB)
        for ic in range(c.IC):
            for ns2 in range(nsb):
                for dc in range(c.DC):
                    p.mm(psC[:, 0:wsb_],
                         ws1_sb[:, dc * c.I + ic * 128:dc * c.I + (ic + 1) * 128],
                         xb[:, dc * c.NB + ns2 * wsb_:dc * c.NB + (ns2 + 1) * wsb_],
                         start=(dc == 0), stop=(dc == c.DC - 1),
                         sem="mm" if dc == c.DC - 1 else None,
                         wait=[("lwc", mws), ("ldx", mxb), p.bkw("C")]
                         if dc == 0 else None)
                mg_ = p.mark("mm")
                for dc in range(c.DC):
                    p.mm(psD[:, 0:wsb_],
                         ws3_sb[:, dc * c.I + ic * 128:dc * c.I + (ic + 1) * 128],
                         xb[:, dc * c.NB + ns2 * wsb_:dc * c.NB + (ns2 + 1) * wsb_],
                         start=(dc == 0), stop=(dc == c.DC - 1),
                         sem="mm" if dc == c.DC - 1 else None,
                         wait=[p.bkw("D")] if dc == 0 else None)
                mu_ = p.mark("mm")
                ms = p.s("activation", silg[:, 0:wsb_], psC[:, 0:wsb_], AF.Silu,
                         sem="evs",
                         wait=[("mm", mg_), ("evv", p.scratch.get("shsil", 0))])
                p.bks("C", "evs", ms)
                mh = p.v("tensor_tensor",
                         hTblk[:, ic * c.NB + ns2 * wsb_:ic * c.NB + (ns2 + 1) * wsb_],
                         silg[:, 0:wsb_], psD[:, 0:wsb_], ALU.mult, sem="evv",
                         wait=[("mm", mu_), ("evs", ms)])
                p.scratch["shsil"] = mh
                p.bks("D", "evv", mh)
        mhdone = p.mark("evv")
        p.scratch["xnblk1"] = p.mark("mm")
        for fc in range(c.DC):
            for ns2 in range(nsb):
                ps = psC if fc % 2 == 0 else psD
                bkey = "C" if fc % 2 == 0 else "D"
                for ic in range(c.IC):
                    p.mm(ps[:, 0:wsb_],
                         ws2_sb[:, ic * c.D + fc * 128:ic * c.D + (fc + 1) * 128],
                         hTblk[:, ic * c.NB + ns2 * wsb_:ic * c.NB + (ns2 + 1) * wsb_],
                         start=(ic == 0), stop=(ic == c.IC - 1),
                         sem="mm" if ic == c.IC - 1 else None,
                         wait=[("lwc", mws2), ("evv", mhdone), p.bkw(bkey),
                               ("ssh", p.scratch.get("shst", 0))]
                         if ic == 0 else None)
                mo_ = p.mark("mm")
                mev = p.v("tensor_copy", shev[:, 0:wsb_], ps[:, 0:wsb_],
                          sem="evv", wait=[("mm", mo_)])
                p.bks(bkey, "evv", mev)
                mshst = p.dma(
                    "sync",
                    t.shoutdT.ap()[fc * 128:(fc + 1) * 128,
                                   blk * c.NB + ns2 * wsb_:blk * c.NB + (ns2 + 1) * wsb_],
                    shev[:, 0:wsb_], "ssh", wait=[("evv", mev)])
                p.scratch["shst"] = mshst
    mshared = p.mark("mm")

    mcc2 = p.mark("cc2")
    mssh_e = p.mark("ssh")
    for e2 in range(2):
        tok_sb = t.tokA_sb if e2 == 0 else t.tokB_sb
        for a in range(c.A):
            par = a % 2
            gas = "ga0" if par == 0 else "ga1"
            relg = p.scratch.get(("gxrel", par), mshared)
            mz = p.v("memset", gx[par], 0.0, sem="evv",
                     wait=[("mm", relg), ("ssh", mssh_e)])
            mgat = p.g("indirect_dma_start", out=gx[par], out_offset=None,
                       in_=t.xnout.ap(),
                       in_offset=IndirectOffsetOnAxis(tok_sb.ap()[:, a:a + 1], 0),
                       sem=gas,
                       wait=[("evv", mz), ("cc2", mcc2), ("gtk", mtok)])
            for gtr in range(0, c.DC, 8):
                n8 = min(8, c.DC - gtr)
                for k8 in range(n8):
                    dc = gtr + k8
                    p.tr(psE[:, k8 * 128:(k8 + 1) * 128],
                         gx[par][:, dc * 128:(dc + 1) * 128], cs["identb"],
                         sem="mm" if k8 == n8 - 1 else None,
                         wait=[(gas, mgat), p.bkw("E"), ("mm", mshared)]
                         if k8 == 0 else None)
                mtr8 = p.mark("mm")
                dst = bufT3[:, gtr:gtr + n8, a * 128:(a + 1) * 128]
                mevE = p.v("tensor_copy", dst, psE[:, 0:n8 * 128], sem="evv",
                           wait=[("mm", mtr8)])
                p.bks("E", "evv", mevE)
            p.scratch[("gxrel", par)] = p.mark("mm")
        mbufT = p.mark("evv")
        ncb = max(c.CAP // c.CW, 1)
        for ic in range(c.IC):
            par = ic % 2
            lwe = "lwa" if par == 0 else "lwb"
            wrel = p.scratch.get(("w13rel", par), mshared)
            mw13 = p.dma("gpsimd", w13_b[par], t.w13p.ap()[e2, ic], lwe,
                         wait=[("mm", wrel), ("ssh", mssh_e)])
            for ns2 in range(ncb):
                for dc in range(c.DC):
                    p.mm(psC[:, 0:c.CW], w13_b[par][:, dc * 128:(dc + 1) * 128],
                         bufT[:, dc * c.CAP + ns2 * c.CW:dc * c.CAP + (ns2 + 1) * c.CW],
                         start=(dc == 0), stop=(dc == c.DC - 1),
                         sem="mm" if dc == c.DC - 1 else None,
                         wait=[(lwe, mw13), ("evv", mbufT), p.bkw("C")]
                         if dc == 0 else None)
                mg_ = p.mark("mm")
                for dc in range(c.DC):
                    p.mm(psD[:, 0:c.CW],
                         w13_b[par][:, c.DC * 128 + dc * 128:c.DC * 128 + (dc + 1) * 128],
                         bufT[:, dc * c.CAP + ns2 * c.CW:dc * c.CAP + (ns2 + 1) * c.CW],
                         start=(dc == 0), stop=(dc == c.DC - 1),
                         sem="mm" if dc == c.DC - 1 else None,
                         wait=[p.bkw("D")] if dc == 0 else None)
                mu_ = p.mark("mm")
                ms = p.s("activation", sile[:, 0:c.CW], psC[:, 0:c.CW], AF.Silu,
                         sem="evs",
                         wait=[("mm", mg_), ("evv", p.scratch.get("esil", 0))])
                p.bks("C", "evs", ms)
                mh = p.v("tensor_tensor",
                         hTe[:, ic * c.CAP + ns2 * c.CW:ic * c.CAP + (ns2 + 1) * c.CW],
                         sile[:, 0:c.CW], psD[:, 0:c.CW], ALU.mult, sem="evv",
                         wait=[("mm", mu_), ("evs", ms)])
                p.scratch["esil"] = mh
                p.bks("D", "evv", mh)
            p.scratch[("w13rel", par)] = p.mark("mm")
        mhe = p.mark("evv")
        for ns2 in range(ncb):
            for fc in range(c.DC):
                par = fc % 2
                lwe2 = "lw2a" if par == 0 else "lw2b"
                wrel = p.scratch.get(("ew2rel", par), mshared)
                mw2 = p.dma("gpsimd", w2_b[par], t.w2q.ap()[e2, fc], lwe2,
                            wait=[("mm", wrel), ("ssh", mssh_e)])
                ps = psC if fc % 2 == 0 else psD
                bkey = "C" if fc % 2 == 0 else "D"
                for ic in range(c.IC):
                    p.mm(ps[:, 0:c.CW], w2_b[par][:, ic * 128:(ic + 1) * 128],
                         hTe[:, ic * c.CAP + ns2 * c.CW:ic * c.CAP + (ns2 + 1) * c.CW],
                         start=(ic == 0), stop=(ic == c.IC - 1),
                         sem="mm" if ic == c.IC - 1 else None,
                         wait=[(lwe2, mw2), ("evv", mhe), p.bkw(bkey)]
                         if ic == 0 else None)
                mo_ = p.mark("mm")
                p.scratch[("ew2rel", par)] = mo_
                mev = p.v("tensor_copy", eev[:, 0:c.CW], ps[:, 0:c.CW],
                          sem="evv",
                          wait=[("mm", mo_), ("trp", p.scratch.get("etrp", 0))])
                p.bks(bkey, "evv", mev)
                nsub = c.CW // 128
                for sub in range(nsub):
                    p.dma("scalar",
                          epan[:, sub * c.D + fc * 128:sub * c.D + (fc + 1) * 128],
                          eev[:, sub * 128:(sub + 1) * 128], "trp",
                          transpose=True,
                          wait=[("evv", mev),
                                ("out", p.scratch.get("epanst", 0))]
                          if sub == 0 else None)
                p.scratch["etrp"] = p.mark("trp")
            metrp = p.mark("trp")
            nsub = c.CW // 128
            for sub in range(nsub):
                row0 = e2 * c.CAP + ns2 * c.CW + sub * 128
                p.dma("sync", t.outin.ap()[row0:row0 + 128, :],
                      epan[:, sub * c.D:(sub + 1) * c.D], "out",
                      wait=[("trp", metrp)] if sub == 0 else None)
            p.scratch["epanst"] = p.mark("out")
            qr = e2 * c.NCB + ns2
            p.g("collective_compute", "AllGather", ALU.bypass,
                replica_groups=[list(range(c.NC))],
                ins=[t.outin.ap()[e2 * c.CAP + ns2 * c.CW:
                                  e2 * c.CAP + (ns2 + 1) * c.CW, :].opt()],
                outs=[t.outout.ap()[qr * c.NC * c.CW:
                                    (qr + 1) * c.NC * c.CW, :].opt()],
                sem="cc3", n=1, wait=[("out", p.mark("out"))])
    mcc3 = p.mark("cc3")

    msao = p.mark("sao")
    mssh = p.mark("ssh")
    for jj in range(c.NT):
        relA = p.scratch.get("cmbA", 0)
        mx = p.dma("sync", xt2, t.x_loc.ap()[jj * 128:(jj + 1) * 128, :], "ldx",
                   wait=[("evv", relA)])
        hD = c.D // 2
        for hh in range(2):
            p.dma("sync", aot[:, hh * hD:(hh + 1) * hD],
                  t.aoutdT.ap()[hh * hD:(hh + 1) * hD, jj * 128:(jj + 1) * 128],
                  "trp", transpose=True,
                  wait=[("sao", msao), ("ssh", mssh)] if (jj == 0 and hh == 0)
                  else None)
            p.dma("sync", sht[:, hh * hD:(hh + 1) * hD],
                  t.shoutdT.ap()[hh * hD:(hh + 1) * hD, jj * 128:(jj + 1) * 128],
                  "trp", transpose=True)
        mtl = p.mark("trp")
        p.wait("vector", "ldx", mx)
        p.wait("vector", "trp", mtl)
        p.wait("vector", "oA", p.scratch.get("cmbAst", 0))
        p.vg("tensor_add", acc, xt2, aot)
        mrel0 = p.vg("tensor_add", acc, acc, sht)
        p.scratch["cmbA"] = mrel0
        mcst = p.dma("sync", t.out_loc.ap()[jj * 128:(jj + 1) * 128, :], acc,
                     "oA", wait=[("evv", mrel0)])
        p.scratch["cmbAst"] = mcst
    mAdone = p.mark("oA")
    mexp_done = p.mark("mm")
    for jj in range(c.NT):
        k = jj % NSET
        g0, g1, xt2b, yv = g0s[k], g1s[k], xt2s[k], yvs[k]
        gbk, lok, obk = "gb%d" % k, "lo%d" % k, "ob%d" % k
        relg = p.scratch.get(("cmbBg", k), 0)
        p.g("indirect_dma_start", out=g0, out_offset=None,
            in_=t.outout.ap(),
            in_offset=IndirectOffsetOnAxis(t.r1l.ap()[:, jj:jj + 1], 0),
            sem=gbk,
            wait=[("cc3", mcc3), ("gat", mloc), ("mm", mexp_done)]
            if jj == 0 else ([("evv", relg)] if jj >= NSET else None))
        p.g("indirect_dma_start", out=g1, out_offset=None,
            in_=t.outout.ap(),
            in_offset=IndirectOffsetOnAxis(t.r2l.ap()[:, jj:jj + 1], 0),
            sem=gbk)
        mgb = p.mark(gbk)
        rell = p.scratch.get(("cmbBl", k), 0)
        p.dma("sync", xt2b, t.out_loc.ap()[jj * 128:(jj + 1) * 128, :], lok,
              wait=[("oA", mAdone)] if jj == 0 else
                   ([("evv", rell)] if jj >= NSET else None))
        mld = p.mark(lok)
        p.vg("tensor_scalar_mul", yv, g0, t.wv1l.ap()[:, jj:jj + 1],
             wait=[(gbk, mgb), (obk, p.scratch.get(("cmbBst", k), 0))])
        mrelg = p.vg("scalar_tensor_tensor", yv, g1,
                     t.wv2l.ap()[:, jj:jj + 1], yv,
                     op0=ALU.mult, op1=ALU.add)
        p.scratch[("cmbBg", k)] = mrelg
        mrel0 = p.vg("tensor_add", yv, yv, xt2b, wait=[(lok, mld)])
        p.scratch[("cmbBl", k)] = mrel0
        mcst = p.dma("scalar", t.out_loc.ap()[jj * 128:(jj + 1) * 128, :],
                     yv, obk, wait=[("evv", mrel0)])
        p.scratch[("cmbBst", k)] = mcst
    p.wait("sync", "oA", p.mark("oA"))
    for k in range(NSET):
        p.wait("sync", "ob%d" % k, p.mark("ob%d" % k))
    mout = p.mark("out")
    for name, (src, dstt) in t.dbg.items():
        p.dma("gpsimd", dstt.ap(), src.ap(), "gat", wait=[("out", mout)])
    mdbg = p.mark("gat")
    p.wait("gpsimd", "gat", mdbg)
    p.wait("sync", "out", mout)



def build(cfg, dbg=()):
    c = cfg
    nc = bass.Bass("TRN2", target_bir_lowering=False)
    t = NS()
    t.ARENA = 88064 if c.D >= 2048 else (40960 if c.S >= 1024 else 26624)

    def ein(name, shape, dtype=F32):
        return nc.dram_tensor(name, list(shape), dtype, kind="ExternalInput")
    t.x_loc = ein("x_loc", (c.TL, c.D))
    t.wqkvp = ein("wqkvp", (c.H, 3, 128, c.D), BF16)
    t.woT = ein("woT", (c.D, c.D), BF16)
    t.ws1T = ein("ws1T", (c.D, c.I), BF16)
    t.ws3T = ein("ws3T", (c.D, c.I), BF16)
    t.ws2T = ein("ws2T", (c.I, c.D), BF16)
    t.w13p = ein("w13p", (2, c.IC, 128, 2 * c.D), BF16)
    t.w2q = ein("w2q", (2, c.DC, 128, c.I), BF16)
    t.cmeta = ein("cmeta", (1, 8), I32)

    const_shapes = {
        "identf": ((128, 128), F32), "identb": ((128, 128), BF16),
        "tmat": ((128, 128), BF16), "tmatf": ((128, 256), F32),
        "iotaCAP": ((128, c.CAP), F32), "iotaE2": ((128, 2 * c.E), F32),
        "cmpT": ((128, 2 * c.JT), BF16), "maskd": ((128, 128), F32),
        "gwT": ((128, c.DC * c.E), F32), "gateb": ((128, c.E), F32),
        "selAB": ((128, 2 * c.E), F32), "onesrow": ((1, 128), F32),
        "onescol_b": ((128, 1), BF16), "coef2": ((2, 1), F32),
        "actc": ((128, 8), F32),
        "maskT4b": ((128, c.QB * c.KCB), BF16),
        "onesrow_b": ((1, 128), BF16),
        "ones2b": ((2, 128), BF16),
    }
    t.const_names = list(const_shapes.keys())
    t.cdr = {}
    for name, (shp, dt) in const_shapes.items():
        t.cdr[name] = ein("c_" + name, shp, dt)

    t.xnT_d = nc.dram_tensor("xnT_d", [c.D, c.TL], BF16)
    t.xnin = nc.dram_tensor("xnin", [c.TL, c.D], BF16)
    t.xnout = nc.dram_tensor("xnout", [c.T, c.D], BF16)
    t.lgin = nc.dram_tensor("lgin", [128, c.NT * c.E], F32)
    t.lgout = nc.dram_tensor("lgout", [c.NC * 128, c.NT * c.E], F32)
    t.attnT_d = nc.dram_tensor("attnT_d", [c.D, c.TL], BF16)
    t.aoutdT = nc.dram_tensor("aoutdT", [c.D, c.TL], BF16)
    t.shoutdT = nc.dram_tensor("shoutdT", [c.D, c.TL], BF16)
    t.totd = nc.dram_tensor("totd", [c.JTE], F32)
    t.based = nc.dram_tensor("based", [c.JTE], F32)
    t.r1d = nc.dram_tensor("r1d", [128, c.JT], I32)
    t.r2d = nc.dram_tensor("r2d", [128, c.JT], I32)
    t.wv1d = nc.dram_tensor("wv1d", [128, c.JT], F32)
    t.wv2d = nc.dram_tensor("wv2d", [128, c.JT], F32)
    t.tokd = nc.dram_tensor("tokd", [2, c.CAP], I32)
    t.outin = nc.dram_tensor("outin", [2 * c.CAP, c.D], BF16)
    t.outout = nc.dram_tensor("outout", [c.E * c.CAP, c.D], BF16)
    t.out_loc = nc.dram_tensor("out_loc", [c.TL, c.D], F32,
                               kind="ExternalOutput")
    t.dbg = {}
    t.dbgst = None
    t.dbgrt = None
    t.dbgsco = None
    t.dbgsg = None
    if "ST" in dbg:
        dbg = tuple(d for d in dbg if d != "ST")
        t.dbgst = nc.dram_tensor("dbg_st", [128, c.NT * 16], F32,
                                 kind="ExternalOutput")
    if "RT" in dbg:
        dbg = tuple(d for d in dbg if d != "RT")
        t.dbgrt = nc.dram_tensor("dbg_rt", [128, 18 * c.JT], F32,
                                 kind="ExternalOutput")
        t.dbgsco = nc.dram_tensor("dbg_sco", [128, c.JTE], F32,
                                  kind="ExternalOutput")
        t.dbgsg = nc.dram_tensor("dbg_sg", [128, c.JTE], F32,
                                 kind="ExternalOutput")
    for name in dbg:
        src = getattr(t, name)
        t.dbg[name] = (src, nc.dram_tensor(
            "dbg_" + name, list(src.ap().shape), src.ap().dtype,
            kind="ExternalOutput"))

    sem_names = ["ldc", "ldx", "ldw", "mm", "evv", "evs", "trp", "gat",
                 "stp0", "sta", "out", "cc1", "cc2", "cc3",
                 "lx0", "lx1", "lwa", "lwb", "lwc", "sxn", "sxn1", "ga0", "ga1",
                 "sao", "ssh", "sro", "sto", "lw2a", "lw2b", "gtk", "stp1",
                 "oA", "oB", "gb0", "gb1", "gb2", "gb3",
                 "lo0", "lo1", "lo2", "lo3", "ob0", "ob1", "ob2", "ob3"]

    import contextlib
    with contextlib.ExitStack() as ctx:
        t.AR = ctx.enter_context(nc.sbuf_tensor("AR", [128, t.ARENA], BF16))
        t.csb = {}
        for name, (shp, dt) in const_shapes.items():
            hnd = ctx.enter_context(nc.sbuf_tensor("cs_" + name, list(shp), dt))
            t.csb[name] = hnd.ap()
        t.cmeta_sb = ctx.enter_context(nc.sbuf_tensor("cmeta_sb", [1, 8], I32))
        actvals = [1e-5, -1.0, 1.0, c.SCALE]
        for vi, vv in enumerate(actvals):
            nc.const_aps.register(F32, vv, t.csb["actc"][:, vi:vi + 1])
        t.r1l = ctx.enter_context(nc.sbuf_tensor("r1l", [128, c.NT], I32))
        t.r2l = ctx.enter_context(nc.sbuf_tensor("r2l", [128, c.NT], I32))
        t.wv1l = ctx.enter_context(nc.sbuf_tensor("wv1l", [128, c.NT], F32))
        t.wv2l = ctx.enter_context(nc.sbuf_tensor("wv2l", [128, c.NT], F32))
        t.tokA_sb = ctx.enter_context(nc.sbuf_tensor("tokA_sb", [128, c.A], I32))
        t.tokB_sb = ctx.enter_context(nc.sbuf_tensor("tokB_sb", [128, c.A], I32))
        t.psA = ctx.enter_context(nc.psum_tensor("psA", [128, 2048], F32))
        t.psB = ctx.enter_context(nc.psum_tensor("psB", [128, 512], F32))
        t.psC = ctx.enter_context(nc.psum_tensor("psC", [128, 512], F32))
        t.psD = ctx.enter_context(nc.psum_tensor("psD", [128, 512], F32))
        t.psE = ctx.enter_context(nc.psum_tensor("psE", [128, 1024], BF16))
        t.sems = {}
        for sn in sem_names:
            t.sems[sn] = ctx.enter_context(nc.semaphore(sn))
        with nc.Block() as block:
            @block.sync
            def _(eng):
                schedule(P(nc, c, t, "sync", {"sync": eng}))

            @block.vector
            def _(eng):
                schedule(P(nc, c, t, "vector", {"vector": eng}))

            @block.scalar
            def _(eng):
                schedule(P(nc, c, t, "scalar", {"scalar": eng}))

            @block.tensor
            def _(eng):
                schedule(P(nc, c, t, "tensor", {"tensor": eng}))

            @block.gpsimd
            def _(eng):
                schedule(P(nc, c, t, "gpsimd", {"gpsimd": eng}))
    return nc


def make_consts(c):
    r = np.arange(128, dtype=np.float32)
    tincl = (r[:, None] <= r[None, :]).astype(np.float32)
    tstrict = (r[:, None] < r[None, :]).astype(np.float32)
    tmatf = np.concatenate([tstrict, np.ones((128, 128), np.float32)], axis=1)
    cmpT = np.zeros((128, 2 * c.JT), np.float32)
    for j in range(c.JT):
        cmpT[:, 2 * j] = j
        cmpT[:, 2 * j + 1] = r
    maskd = np.where(r[None, :] <= r[:, None], 0.0, NEG).astype(np.float32)
    QB = c.QB
    KCB = c.KCB
    kp = np.arange(128)
    qf = np.arange(QB)
    mT4 = np.zeros((128, QB * KCB), np.float32)
    for d in range(KCB):
        mT4[:, d * QB:(d + 1) * QB] = (qf[None, :] >= kp[:, None] + d * 128)
    iotaE2 = np.concatenate([
        np.tile(np.arange(c.E, dtype=np.float32), (128, 1)),
        np.tile(np.arange(c.E - 1, -1, -1, dtype=np.float32), (128, 1))],
        axis=1)
    return {
        "identf": np.eye(128, dtype=np.float32),
        "identb": np.eye(128, dtype=np.float32).astype(ml_dtypes.bfloat16),
        "tmat": tincl.astype(ml_dtypes.bfloat16),
        "tmatf": tmatf,
        "iotaCAP": np.tile(np.arange(c.CAP, dtype=np.float32), (128, 1)),
        "iotaE2": iotaE2,
        "cmpT": cmpT.astype(ml_dtypes.bfloat16),
        "maskd": maskd,
        "onesrow": np.ones((1, 128), np.float32),
        "onescol_b": np.ones((128, 1), np.float32).astype(ml_dtypes.bfloat16),
        "coef2": np.array([[128.0], [1.0]], np.float32),
        "actc": np.tile(np.array([[1e-5, -1.0, 1.0, 1.0 / math.sqrt(128.0),
                           0.0, 0.0, 0.0, 0.0]], np.float32), (128, 1)),
        "maskT4b": mT4.astype(ml_dtypes.bfloat16),
        "onesrow_b": np.ones((1, 128), np.float32).astype(ml_dtypes.bfloat16),
        "ones2b": np.ones((2, 128), np.float32).astype(ml_dtypes.bfloat16),
    }


def make_inmaps(c, inputs):
    x = np.asarray(inputs["L0_attn_x"], np.float32).reshape(c.T, c.D)
    gw = np.asarray(inputs["L0_ffn_gate_w"], np.float32)
    gb = np.asarray(inputs["L0_ffn_gate_b"], np.float32)
    W1 = np.asarray(inputs["L0_ffn_W1"], np.float32)
    W3 = np.asarray(inputs["L0_ffn_W3"], np.float32)
    W2 = np.asarray(inputs["L0_ffn_W2"], np.float32)

    consts = make_consts(c)
    gwT_packed = np.ascontiguousarray(gw.T).reshape(c.DC, 128, c.E) \
        .transpose(1, 0, 2).reshape(128, c.DC * c.E).copy()
    shared = {
        "wqkvp": np.ascontiguousarray(np.stack([
            np.asarray(inputs["L0_attn_" + nm], np.float32)
            .reshape(c.H, 128, c.DC, 128).transpose(0, 3, 2, 1)
            .reshape(c.H, 128, c.D)
            for nm in ("wq", "wk", "wv")], axis=1)).astype(ml_dtypes.bfloat16),
        "woT": np.ascontiguousarray(
            np.asarray(inputs["L0_attn_wo"], np.float32).T).astype(
                ml_dtypes.bfloat16),
        "ws1T": np.ascontiguousarray(
            np.asarray(inputs["L0_ffn_Ws1"], np.float32).T).astype(
                ml_dtypes.bfloat16),
        "ws3T": np.ascontiguousarray(
            np.asarray(inputs["L0_ffn_Ws3"], np.float32).T).astype(
                ml_dtypes.bfloat16),
        "ws2T": np.ascontiguousarray(
            np.asarray(inputs["L0_ffn_Ws2"], np.float32).T).astype(
                ml_dtypes.bfloat16),
        "c_gwT": gwT_packed,
        "c_gateb": np.tile(gb[None, :], (128, 1)).astype(np.float32),
    }
    for k, v in consts.items():
        shared["c_" + k] = v
    in_maps = []
    for core in range(c.NC):
        m = dict(shared)
        m["x_loc"] = np.ascontiguousarray(x[core * c.TL:(core + 1) * c.TL])
        eids = [core, core + c.NC]
        w1p = W1[eids].reshape(2, c.IC, 128, c.DC, 128) \
            .transpose(0, 1, 4, 3, 2).reshape(2, c.IC, 128, c.D)
        w3p = W3[eids].reshape(2, c.IC, 128, c.DC, 128) \
            .transpose(0, 1, 4, 3, 2).reshape(2, c.IC, 128, c.D)
        m["w13p"] = np.ascontiguousarray(
            np.concatenate([w1p, w3p], axis=-1)).astype(ml_dtypes.bfloat16)
        m["w2q"] = np.ascontiguousarray(
            W2[eids].reshape(2, c.DC, 128, c.IC, 128)
            .transpose(0, 1, 4, 3, 2).reshape(2, c.DC, 128, c.I)) \
            .astype(ml_dtypes.bfloat16)
        sel = np.zeros((128, 2 * c.E), np.float32)
        sel[:, eids[0]] = 1.0
        sel[:, c.E + eids[1]] = 1.0
        m["c_selAB"] = sel
        cm = np.zeros((1, 8), np.int32)
        cm[0, 0] = core
        m["cmeta"] = cm
        in_maps.append(m)
    return in_maps


def run_cfg(c, inputs, dbg=(), want_raw=False):
    nc = build(c, dbg=dbg)
    in_maps = make_inmaps(c, inputs)
    res = run_bass_kernel_spmd(nc, in_maps, core_ids=list(range(c.NC)))
    out = np.concatenate([r["out_loc"] for r in res.results], axis=0)
    out = out.reshape(c.B, c.S, c.D)
    if want_raw or dbg:
        return out, res.results
    return out


def kernel(**inputs):
    return run_cfg(cfg_full(), inputs)


# version 20 (final)
# speedup vs baseline: 1.0208x; 1.0208x over previous
import math
import numpy as np
import ml_dtypes
import concourse.bass as bass
import concourse.mybir as mybir
from concourse.bass_utils import run_bass_kernel_spmd
from concourse.bass_core_rust import IndirectOffsetOnAxis

F32 = mybir.dt.float32
BF16 = mybir.dt.bfloat16
I32 = mybir.dt.int32
AF = mybir.ActivationFunctionType
ALU = mybir.AluOpType
AX = mybir.AxisListType

NEG = -1.0e30


class Cfg:
    def __init__(self, B, S, D, H, E, CAP, I, NC=8):
        self.B, self.S, self.D, self.H, self.E, self.CAP, self.I, self.NC = \
            B, S, D, H, E, CAP, I, NC
        self.HD = D // H
        assert self.HD == 128
        self.T = B * S
        self.BL = B // NC
        self.TL = self.T // NC
        self.NT = self.TL // 128
        self.JT = self.T // 128
        self.DC = D // 128
        self.IC = I // 128
        self.SC = S // 128
        self.A = max(CAP // 128, 1)
        self.LA = int(math.log2(self.A))
        self.LCAP = int(math.log2(CAP))
        self.JTE = self.JT * E
        self.NB = min(512, self.TL)
        self.NBLK = self.TL // self.NB
        self.SCALE = 1.0 / math.sqrt(self.HD)
        self.PGRP = 4 if self.NT % 4 == 0 else 1
        self.SW = min(512, self.S)
        self.CW = min(512, CAP)
        self.LCW = int(math.log2(self.CW))
        self.LNC = int(math.log2(NC))
        self.NCB = max(CAP // self.CW, 1)
        self.LNCB = int(math.log2(self.NCB))
        self.QB = min(512, self.S)
        self.KCB = self.QB // 128
        self.EOFF = max(CAP, 512)


def cfg_full():
    return Cfg(B=16, S=2048, D=2048, H=16, E=16, CAP=1024, I=1408)


def cfg_small():
    return Cfg(B=8, S=256, D=256, H=2, E=16, CAP=128, I=256)


ENGS = ("sync", "vector", "scalar", "tensor", "gpsimd")


class NS:
    pass


def view(t, off, shape, dtype=BF16):
    ncols = 1
    for d in shape[1:]:
        ncols *= d
    mult = {BF16: 1, F32: 2, I32: 2}[dtype]
    assert off % 2 == 0
    ap = t.ap()[:, off:off + ncols * mult]
    if dtype != BF16:
        ap = ap.bitcast(dtype)
    if len(shape) == 3:
        ap = ap.rearrange("p (a b) -> p a b", b=shape[2])
    return ap


class P:
    def __init__(self, nc, cfg, t, cur, eng_objs):
        self.nc, self.cfg, self.t, self.cur = nc, cfg, t, cur
        self.E = eng_objs
        self.cnt = {}
        self.scratch = {}

    def run(self, eng, fname, *a, **kw):
        if self.cur == eng:
            return getattr(self.E[eng], fname)(*a, **kw)
        return None

    def inc(self, inst, sem, n):
        self.cnt[sem] = self.cnt.get(sem, 0) + n
        if inst is not None:
            inst.then_inc(self.t.sems[sem], n)
        return self.cnt[sem]

    def mark(self, sem):
        return self.cnt.get(sem, 0)

    def wait(self, eng, sem, val):
        if val is None or val <= 0:
            return
        if self.cur == eng:
            self.E[eng].wait_ge(self.t.sems[sem], val)

    def waits(self, eng, wlist):
        if wlist:
            for (s, v) in wlist:
                self.wait(eng, s, v)

    def dma(self, eng, out, in_, sem, wait=None, **kw):
        self.waits(eng, wait)
        i = self.run(eng, "dma_start", out=out, in_=in_, **kw)
        return self.inc(i, sem, 16)

    def mm(self, out, lhsT, rhs, start, stop, sem=None, wait=None):
        self.waits("tensor", wait)
        i = self.run("tensor", "matmul", out, lhsT, rhs, start=start, stop=stop)
        if sem is not None:
            return self.inc(i, sem, 1)
        return None

    def tr(self, out, in_, ident, sem=None, wait=None):
        self.waits("tensor", wait)
        i = self.run("tensor", "transpose", out, in_, ident)
        if sem is not None:
            return self.inc(i, sem, 1)
        return None

    def v(self, fname, *a, sem=None, wait=None, **kw):
        self.waits("vector", wait)
        i = self.run("vector", fname, *a, **kw)
        if sem is not None:
            return self.inc(i, sem, 1)
        return None

    def vg(self, fname, *a, wait=None, **kw):
        m = self.v(fname, *a, sem="evv", wait=wait, **kw)
        self.wait("vector", "evv", m)
        return m

    def s(self, fname, *a, sem=None, wait=None, **kw):
        self.waits("scalar", wait)
        i = self.run("scalar", fname, *a, **kw)
        if sem is not None:
            return self.inc(i, sem, 1)
        return None

    def bkw(self, key):
        return self.scratch.get(("bk", key), ("evv", 0))

    def bks(self, key, sem, mark):
        self.scratch[("bk", key)] = (sem, mark)

    def sg(self, fname, *a, wait=None, **kw):
        m = self.s(fname, *a, sem="evs", wait=wait, **kw)
        self.wait("scalar", "evs", m)
        return m

    def g(self, fname, *a, sem=None, n=16, wait=None, **kw):
        self.waits("gpsimd", wait)
        i = self.run("gpsimd", fname, *a, **kw)
        if sem is not None:
            return self.inc(i, sem, n)
        return None



def schedule(p: P):
    c = p.cfg
    t = p.t
    AR = t.AR
    nc = p.nc
    sc_ = c.SCALE

    NSM = 28
    SMB = t.ARENA - NSM * (c.JT * 2)

    o = 0
    xtile = [view(AR, o, [128, c.D], F32), view(AR, o + 2 * c.D, [128, c.D], F32)]
    o += 4 * c.D
    sq = view(AR, o, [128, c.D], F32); o += 2 * c.D
    xnf = [view(AR, o, [128, c.D], F32), view(AR, o + 2 * c.D, [128, c.D], F32)]
    o += 4 * c.D
    xnb = [view(AR, o, [128, c.D], BF16), view(AR, o + c.D, [128, c.D], BF16)]
    o += 2 * c.D
    xnch = [view(AR, o, [128, 512], F32), view(AR, o + 1024, [128, 512], F32)]
    o += 2048
    panelT = []
    for g in range(2):
        panelT.append(view(AR, o, [128, c.DC * 128 * c.PGRP], BF16))
        o += c.DC * 128 * c.PGRP
    logit_sb = view(AR, o, [128, c.NT * c.E], F32); o += c.NT * c.E * 2
    st_ = view(AR, o, [128, 16], F32); o += 32
    assert o <= SMB, ("p0", o, SMB)

    o = 0
    xnTb = view(AR, o, [128, c.DC * c.S], BF16)
    woT_sb = view(AR, o, [128, c.DC * c.D], BF16)
    o += max(c.DC * c.S, c.DC * c.D)
    wq_b, wk_b, wv_b = [], [], []
    for par in range(2):
        wq_b.append(view(AR, o, [128, c.DC * 128], BF16)); o += c.DC * 128
        wk_b.append(view(AR, o, [128, c.DC * 128], BF16)); o += c.DC * 128
        wv_b.append(view(AR, o, [128, c.DC * 128], BF16)); o += c.DC * 128
    QTt = view(AR, o, [128, c.S], BF16); o += c.S
    KTt = view(AR, o, [128, c.S], BF16); o += c.S
    Vtmp = view(AR, o, [128, c.S], BF16); o += c.S
    Vsb = view(AR, o, [128, c.S], BF16); o += c.S
    probsTb = []
    for par in range(2):
        probsTb.append(view(AR, o, [128, c.QB], BF16)); o += c.QB
    crow = view(AR, o, [128, c.S], BF16)[0:1, :]; o += c.S
    negmb = view(AR, o, [128, 2], BF16); o += 2
    zrow = view(AR, o, [128, 512], F32)[0:1, :]; o += 1024
    recb = view(AR, o, [128, c.QB], F32); o += 2 * c.QB
    avpan = view(AR, o, [128, 512], BF16); o += 512
    smb = view(AR, o, [128, 8], F32); o += 16
    atb = view(AR, o, [128, c.DC * c.SW], BF16); o += c.DC * c.SW
    oevac = view(AR, o, [128, c.SW], BF16); o += c.SW
    assert o <= SMB, ("attn", o, SMB)

    JTE = c.JTE
    B = []
    o = 0
    for k in range(7):
        B.append(view(AR, o, [128, JTE], F32)); o += JTE * 2
    Obf = view(AR, o, [128, JTE], BF16); o += JTE
    trow = view(AR, o, [128, 512], F32); o += 1024
    totch = view(AR, o, [128, 2 * c.E], F32); o += 4 * c.E
    basej = view(AR, o, [128, 2 * c.E], F32); o += 4 * c.E
    rt_end = o
    assert rt_end <= SMB, ("routing", rt_end, SMB)

    def v_je(ap):
        return ap.rearrange("p (j e) -> p j e", e=c.E)

    def v_ej(ap):
        return ap.rearrange("p (j e) -> p e j", e=c.E)

    o = SMB
    def smv(dtype=F32):
        nonlocal o
        r = view(AR, o, [128, c.JT], dtype); o += c.JT * 2
        return r
    m1 = smv(); m2 = smv(); idx1 = smv(); idx2 = smv(); q1 = smv(); q2 = smv()
    w1 = smv(); w2 = smv(); wn1 = smv(); wn2 = smv(); pos1 = smv(); pos2 = smv()
    val1 = smv(); val2 = smv(); mAv = smv(); pAv = smv(); mBv = smv(); pBv = smv()
    wsum = smv(); winv = smv()
    ri1 = smv(I32); ri2 = smv(I32)
    it1 = smv(I32); it2 = smv(I32); it3 = smv(I32); it4 = smv(I32)
    it5 = smv(I32); it6 = smv(I32)
    assert o <= t.ARENA

    o = 0
    Pb = []
    for k in range(4):
        Pb.append(view(AR, o, [128, c.CAP], BF16)); o += c.CAP
    sb2 = view(AR, o, [128, c.CAP], F32); o += c.CAP * 2
    tokrow = view(AR, o, [128, c.CAP], F32); o += c.CAP * 2
    tokrowi = view(AR, o, [128, c.CAP], I32); o += c.CAP * 2
    assert o <= SMB, ('compaction', o, SMB)

    o = 0
    ws1_sb = view(AR, o, [128, c.DC * c.I], BF16); o += c.DC * c.I
    ws3_sb = view(AR, o, [128, c.DC * c.I], BF16); o += c.DC * c.I
    xnTblk = [view(AR, o, [128, c.DC * c.NB], BF16)]; o += c.DC * c.NB
    xnTblk.append(xnTblk[0])
    hTblk = view(AR, o, [128, c.IC * c.NB], BF16); o += c.IC * c.NB
    silg = view(AR, o, [128, 512], BF16); o += 512
    shev = view(AR, o, [128, 512], BF16); o += 512
    ws2_sb = view(AR, o, [128, c.IC * c.D], BF16); o += c.IC * c.D
    assert o <= t.ARENA, ("shared", o, t.ARENA)

    o = 0
    bufT = view(AR, o, [128, c.DC * c.CAP], BF16)
    bufT3 = bufT.rearrange("p (dc s) -> p dc s", s=c.CAP)
    o += c.DC * c.CAP
    hTe = view(AR, o, [128, c.IC * c.CAP], BF16); o += c.IC * c.CAP
    gx = [view(AR, o, [128, c.D], BF16)]; o += c.D
    gx.append(view(AR, o, [128, c.D], BF16)); o += c.D
    w13_b = []
    for par in range(2):
        w13_b.append(view(AR, o, [128, 2 * c.DC * 128], BF16)); o += 2 * c.DC * 128
    w2_b = []
    for par in range(2):
        w2_b.append(view(AR, o, [128, c.IC * 128], BF16)); o += c.IC * 128
    epan = view(AR, o, [128, 4 * c.D], BF16); o += 4 * c.D
    eev = view(AR, o, [128, 512], BF16); o += 512
    sile = view(AR, o, [128, 512], BF16); o += 512
    assert o <= SMB, ("expert", o, SMB)

    NSET = 4
    o = 0
    g0s, g1s, xt2s, yvs = [], [], [], []
    for k in range(NSET):
        g0s.append(view(AR, o, [128, c.D], BF16)); o += c.D
        g1s.append(view(AR, o, [128, c.D], BF16)); o += c.D
        xt2s.append(view(AR, o, [128, c.D], F32)); o += 2 * c.D
        yvs.append(view(AR, o, [128, c.D], F32)); o += 2 * c.D
    assert o <= 56320 if c.D >= 2048 else o <= SMB
    oA = 56320 if c.D >= 2048 else SMB - 6 * c.D
    xt2 = view(AR, oA, [128, c.D], F32); oA += 2 * c.D
    aot = view(AR, oA, [128, c.D], BF16); oA += c.D
    sht = view(AR, oA, [128, c.D], BF16); oA += c.D
    acc = view(AR, oA, [128, c.D], F32); oA += 2 * c.D
    assert oA <= (SMB if c.D >= 2048 else t.ARENA), oA

    psA = t.psA.ap()
    psB = t.psB.ap()
    psC = t.psC.ap()
    psD = t.psD.ap()
    psE = t.psE.ap()
    cs = t.csb

    for name in t.const_names:
        p.dma("sync", cs[name], t.cdr[name].ap(), "ldc")
    p.dma("sync", t.cmeta_sb.ap(), t.cmeta.ap(), "ldc")
    mC = p.mark("ldc")
    for e in ENGS:
        p.wait(e, "ldc", mC)

    tile_rel = {}
    panel_flush = {0: ("stp0", 0), 1: ("stp0", 0)}
    xnb_st = {0: 0, 1: 0}
    xnf_rel = {0: 0, 1: 0}
    xnf_m = {}

    def p0_stats_stages(i):
        bi = i % 2
        lxs = "lx0" if bi == 0 else "lx1"
        q0 = bi * 8
        sst = {}

        def s0():
            relv, rels = tile_rel.get(i - 2, (0, 0))
            w_ld = p.dma("sync", xtile[bi],
                         t.x_loc.ap()[i * 128:(i + 1) * 128, :],
                         lxs, wait=[("evv", relv), ("evs", rels)])
            p.wait("vector", lxs, w_ld)
            p.wait("scalar", lxs, w_ld)
            p.vg("reduce_sum", st_[:, q0:q0 + 1], xtile[bi], AX.X)
            sst['mss'] = p.s("activation", sq, xtile[bi], AF.Square, sem="evs")

        def s1():
            p.wait("vector", "evs", sst['mss'])
            p.vg("reduce_sum", st_[:, q0 + 1:q0 + 2], sq, AX.X)

        def s2():
            p.vg("tensor_scalar_mul", st_[:, q0 + 2:q0 + 3], st_[:, q0:q0 + 1],
                 1.0 / c.D)
            p.vg("tensor_mul", st_[:, q0 + 3:q0 + 4], st_[:, q0 + 2:q0 + 3],
                 st_[:, q0 + 2:q0 + 3])

        def s3():
            mvar = p.vg("scalar_tensor_tensor", st_[:, q0 + 4:q0 + 5],
                        st_[:, q0 + 1:q0 + 2], 1.0 / c.D,
                        st_[:, q0 + 3:q0 + 4],
                        op0=ALU.mult, op1=ALU.subtract)
            sst['msd'] = p.s("activation", st_[:, q0 + 5:q0 + 6],
                             st_[:, q0 + 4:q0 + 5],
                             AF.Sqrt, bias=1e-5, sem="evs",
                             wait=[("evv", mvar)])

        def s4():
            p.vg("reciprocal", st_[:, q0 + 6:q0 + 7], st_[:, q0 + 5:q0 + 6],
                 wait=[("evs", sst['msd'])])

        def s5():
            mxnf = p.v("tensor_scalar", xnf[bi], xtile[bi],
                       st_[:, q0 + 2:q0 + 3], st_[:, q0 + 6:q0 + 7],
                       op0=ALU.subtract, op1=ALU.mult, sem="evv",
                       wait=[("mm", xnf_rel[bi])])
            mxnb = p.v("tensor_scalar", xnb[bi], xtile[bi],
                       st_[:, q0 + 2:q0 + 3], st_[:, q0 + 6:q0 + 7],
                       op0=ALU.subtract, op1=ALU.mult, sem="evv",
                       wait=[("sxn" if bi == 0 else "sxn1", xnb_st[bi])])
            xnf_m[i] = mxnf
            tile_rel[i] = (mxnb, sst['mss'])
            xnb_st[bi] = p.dma("sync", t.xnin.ap()[i * 128:(i + 1) * 128, :],
                               xnb[bi], "sxn" if bi == 0 else "sxn1",
                               wait=[("evv", mxnb)])
            if t.dbgst is not None:
                p.dma("sync", t.dbgst.ap()[:, i * 16:(i + 1) * 16], st_, "sta",
                      wait=[("evv", mxnb)])

        return [s0, s1, s2, s3, s4, s5]

    for f in p0_stats_stages(0):
        f()
    stage_at = {}
    for k_ in range(6):
        stage_at.setdefault(min(c.DC - 1, (k_ * c.DC) // 6), []).append(k_)
    for i in range(c.NT):
        bi = i % 2
        stages = p0_stats_stages(i + 1) if i + 1 < c.NT else None
        mxnf = xnf_m[i]
        gsel = (i // c.PGRP) % 2
        W4 = min(4, c.DC)
        ngr = c.DC // W4
        early5 = ngr >= 4
        gmap = [0, 0, 1, 1, 2] if early5 else [0, 0, 1, 1, 2, 3]
        gr_stage = {}
        for k_ in range(5 if early5 else 6):
            gr_stage.setdefault(min(ngr - 1, (gmap[k_] * ngr) // 4),
                                []).append(k_)
        pT3 = panelT[gsel].rearrange("p (dc w) -> p dc w", w=128 * c.PGRP)
        off = (i % c.PGRP) * 128
        for g in range(ngr):
            pb = g % 2
            pst = psC if pb == 0 else psD
            bkey = "C" if pb == 0 else "D"
            if stages is not None and early5 and g == ngr - 1:
                stages[5]()
            for j in range(W4):
                dc = g * W4 + j
                p.tr(pst[:, j * 128:(j + 1) * 128],
                     xnf[bi][:, dc * 128:(dc + 1) * 128], cs["identf"],
                     sem="mm" if j == W4 - 1 else None,
                     wait=[("evv", mxnf), p.bkw(bkey)] if j == 0 else None)
            mtr = p.mark("mm")
            p.wait("vector", "mm", mtr)
            p.v("tensor_copy", xnch[pb][:, 0:W4 * 128], pst[:, 0:W4 * 128],
                wait=[("mm", p.scratch.get(("lgrel", pb), 0))])
            ps3 = pst[:, 0:W4 * 128].rearrange("p (dc w) -> p dc w", w=128)
            mev = p.v("tensor_copy",
                      pT3[:, g * W4:(g + 1) * W4, off:off + 128],
                      ps3, sem="evv",
                      wait=[panel_flush[gsel]]
                      if (g == 0 and i % c.PGRP == 0) else None)
            p.bks(bkey, "evv", mev)
            for j in range(W4):
                dc = g * W4 + j
                p.mm(psB[:, 0:c.E], xnch[pb][:, j * 128:(j + 1) * 128],
                     cs["gwT"][:, dc * c.E:(dc + 1) * c.E],
                     start=(dc == 0), stop=(dc == c.DC - 1),
                     sem="mm" if j == W4 - 1 else None,
                     wait=[("evv", mev), p.bkw("B")] if dc == 0 else
                          ([("evv", mev)] if j == 0 else None))
            p.scratch[("lgrel", pb)] = p.mark("mm")
            if stages is not None and g in gr_stage:
                for k_ in gr_stage[g]:
                    stages[k_]()
        xnf_rel[bi] = p.mark("mm")
        mlg = p.v("tensor_copy", logit_sb[:, i * c.E:(i + 1) * c.E], psB[:, 0:c.E],
                  sem="evv", wait=[("mm", xnf_rel[bi])])
        p.bks("B", "evv", mlg)
        if (i + 1) % c.PGRP == 0:
            i4 = i // c.PGRP
            mflush = p.mark("evv")
            bsem = "stp0" if i4 % 2 == 0 else "stp1"
            for dc in range(c.DC):
                p.dma(
                    "scalar",
                    t.xnT_d.ap()[dc * 128:(dc + 1) * 128,
                                 i4 * 128 * c.PGRP:(i4 + 1) * 128 * c.PGRP],
                    panelT[i4 % 2][:, dc * 128 * c.PGRP:(dc + 1) * 128 * c.PGRP],
                    bsem, wait=[("evv", mflush)] if dc == 0 else None)
            panel_flush[i4 % 2] = (bsem, p.mark(bsem))
            p.scratch.setdefault("stpb", {})[(i + 1) * 128] = panel_flush[i4 % 2]
    mlgall = p.mark("evv")
    p.dma("sync", t.lgin.ap(), logit_sb, "sro", wait=[("evv", mlgall)])
    mlgst = p.mark("sro")
    mstp0 = p.mark("stp0")
    mstp1 = p.mark("stp1")
    mstp_b = p.scratch.get("stpb", {})
    msxn = p.mark("sxn")
    msxn1 = p.mark("sxn1")
    p.g("collective_compute", "AllGather", ALU.bypass,
        replica_groups=[list(range(c.NC))],
        ins=[t.lgin.ap().opt()], outs=[t.lgout.ap().opt()],
        sem="cc1", n=1, wait=[("sro", mlgst), ("sxn", msxn), ("sxn1", msxn1)])
    p.g("collective_compute", "AllGather", ALU.bypass,
        replica_groups=[list(range(c.NC))],
        ins=[t.xnin.ap().opt()], outs=[t.xnout.ap().opt()],
        sem="cc2", n=1)

    mWfree = {}
    gqt = 0
    for b in range(c.BL):
        mo_done = p.scratch.get("outproj_mm", 0)
        for dc in range(c.DC):
            p.dma("sync", xnTb[:, dc * c.S:(dc + 1) * c.S],
                  t.xnT_d.ap()[dc * 128:(dc + 1) * 128, b * c.S:(b + 1) * c.S],
                  "ldx",
                  wait=[("stp0", mstp0), ("stp1", mstp1), ("sxn", msxn),
                        ("sxn1", msxn1),
                        ("sro", mlgst), ("mm", mo_done)] if dc == 0 else None)
        mxnT = p.mark("ldx")
        for h in range(c.H):
            par = h % 2
            lws = "lwa" if par == 0 else "lwb"
            wfree = mWfree.get((b, h - 2), 0)
            for wi, wsb in enumerate((wq_b[par], wk_b[par], wv_b[par])):
                w = None
                if wi == 0:
                    w = [("mm", wfree)]
                    if b == 0 and h < 2:
                        w += [("stp0", mstp0), ("stp1", mstp1),
                              ("sxn", msxn), ("sxn1", msxn1),
                              ("sro", mlgst)]
                p.dma("gpsimd", wsb, t.wqkvp.ap()[h, wi], lws, wait=w)
            mw = p.mark(lws)
            nblkS = c.S // c.SW

            def qkv(dst, wsb, eng):
                last = None
                for nb in range(nblkS):
                    ps = psC if nb % 2 == 0 else psD
                    bkey = "C" if nb % 2 == 0 else "D"
                    for dc in range(c.DC):
                        w = None
                        if dc == 0:
                            w = [(lws, mw), ("ldx", mxnT), p.bkw(bkey)]
                        p.mm(ps[:, 0:c.SW], wsb[:, dc * 128:(dc + 1) * 128],
                             xnTb[:, dc * c.S + nb * c.SW:dc * c.S + (nb + 1) * c.SW],
                             start=(dc == 0), stop=(dc == c.DC - 1),
                             sem="mm" if dc == c.DC - 1 else None, wait=w)
                    mgrp = p.mark("mm")
                    if eng == "vector":
                        m = p.v("tensor_copy", dst[:, nb * c.SW:(nb + 1) * c.SW],
                                ps[:, 0:c.SW], sem="evv", wait=[("mm", mgrp)])
                        last = ("evv", m)
                    else:
                        m = p.s("activation", dst[:, nb * c.SW:(nb + 1) * c.SW],
                                ps[:, 0:c.SW], AF.Copy, sem="evs",
                                wait=[("mm", mgrp)])
                        last = ("evs", m)
                    p.bks(bkey, *last)
                return last
            sQ, mQ = qkv(QTt, wq_b[par], "vector")
            sK, mK = qkv(KTt, wk_b[par], "scalar")
            sV, mV = qkv(Vtmp, wv_b[par], "vector")
            mWfree[(b, h)] = p.mark("mm")
            mVev = 0
            for grp in range(0, c.SC, 8):
                n8 = min(8, c.SC - grp)
                for k8 in range(n8):
                    scix = grp + k8
                    p.tr(psE[:, k8 * 128:(k8 + 1) * 128],
                         Vtmp[:, scix * 128:(scix + 1) * 128], cs["identb"],
                         sem="mm" if k8 == n8 - 1 else None,
                         wait=[(sV, mV), p.bkw("E")] if k8 == 0 else None)
                mtr8 = p.mark("mm")
                mVev = p.v("tensor_copy", Vsb[:, grp * 128:(grp + n8) * 128],
                           psE[:, 0:n8 * 128], sem="evv", wait=[("mm", mtr8)])
                p.bks("E", "evv", mVev)
            def p1_tail(qt):
                mc = p.vg("tensor_copy", negmb[:, 0:1], smb[:, 4:5])
                mtc = p.tr(psE[0:1, 0:128], negmb[:, 0:1], cs["identb"],
                           sem="mm", wait=[("evv", mc), p.bkw("E")])
                crel = p.v("tensor_copy", crow[0:1, qt * 128:(qt + 1) * 128],
                           psE[0:1, 0:128], sem="evv", wait=[("mm", mtc)])
                p.bks("E", "evv", crel)
                p.scratch["crel"] = crel
            for qt in range(c.SC):
                ext_t = qt + 1
                ext = ext_t * 128
                nch = (ext + 511) // 512
                chmms = []
                for nb2 in range(nch):
                    lo = nb2 * 512
                    hi = min(ext, lo + 512)
                    w = [p.bkw("A%d" % (lo // 512))]
                    if nb2 == 0:
                        w += [("evs", mK), ("evv", mQ)]
                    p.mm(psA[:, lo:hi], QTt[:, qt * 128:(qt + 1) * 128],
                         KTt[:, lo:hi], start=True, stop=True,
                         sem="mm", wait=w)
                    chmms.append(p.mark("mm"))
                if qt > 0:
                    p1_tail(qt - 1)
                p.vg("tensor_add", psA[:, ext - 128:ext], psA[:, ext - 128:ext],
                     cs["maskd"], wait=[("mm", chmms[-1])])
                for nb2 in range(nch):
                    lo = nb2 * 512
                    hi = min(ext, lo + 512)
                    mmx = p.vg("tensor_reduce", smb[:, nb2:nb2 + 1],
                               psA[:, lo:hi], AX.X, ALU.max, negate=True,
                               wait=[("mm", chmms[nb2])] if nb2 < nch - 1 else None)
                    p.bks("A%d" % nb2, "evv", mmx)
                p.vg("tensor_reduce", smb[:, 4:5], smb[:, 0:nch], AX.X, ALU.min)
                if qt == c.SC - 1:
                    p1_tail(qt)
            mcrow = p.scratch.get("crel", 0)
            for qq in range(c.S // c.QB):
                pav = psC if qq % 2 == 0 else psD
                avkey = "C" if qq % 2 == 0 else "D"
                relav = p.bkw(avkey)
                relB = p.bkw("B")
                nkc = (qq + 1) * c.KCB
                pend = {}

                def p2_av(kc):
                    ppar = kc % 2
                    wsem, mexp = pend.pop(kc)
                    p.mm(pav[:, 0:c.QB], Vsb[:, kc * 128:(kc + 1) * 128],
                         probsTb[ppar], start=(kc == 0), stop=(kc == nkc - 1),
                         sem="mm",
                         wait=[(wsem, mexp), ("evv", mVev), relav]
                         if kc == 0 else [(wsem, mexp)])
                    p.mm(psB[0:1, 0:c.QB], cs["onescol_b"], probsTb[ppar],
                         start=(kc == 0), stop=(kc == nkc - 1),
                         sem="mm" if kc == nkc - 1 else None,
                         wait=[relB] if kc == 0 else None)
                    p.scratch[("ptb", ppar)] = p.mark("mm")

                for kc in range(nkc):
                    bA = kc % 4
                    pst = psA[:, bA * 512:bA * 512 + c.QB]
                    p.mm(pst, KTt[:, kc * 128:(kc + 1) * 128],
                         QTt[:, qq * c.QB:(qq + 1) * c.QB],
                         start=True, stop=False,
                         wait=[p.bkw("A%d" % bA), ("evv", mcrow)])
                    p.mm(pst, cs["onesrow_b"],
                         crow[0:1, qq * c.QB:(qq + 1) * c.QB],
                         start=False, stop=True, sem="mm")
                    mscT = p.mark("mm")
                    ppar = kc % 2
                    prel = p.scratch.get(("ptb", ppar), 0)
                    d = kc - qq * c.KCB
                    mexp = p.s("activation", probsTb[ppar], pst, AF.Exp,
                               scale=sc_, sem="evs",
                               wait=[("mm", max(mscT, prel))])
                    p.bks("A%d" % bA, "evs", mexp)
                    if d >= 0:
                        mexp = p.v("tensor_tensor", probsTb[ppar],
                                   probsTb[ppar],
                                   cs["maskT4b"][:, d * c.QB:(d + 1) * c.QB],
                                   ALU.mult, sem="evv", wait=[("evs", mexp)])
                        wsem = "evv"
                    else:
                        wsem = "evs"
                    pend[kc] = (wsem, mexp)
                    if kc > 0:
                        p2_av(kc - 1)
                p2_av(nkc - 1)
                mavT = p.mark("mm")
                mze = p.vg("tensor_copy", zrow[0:1, 0:c.QB], psB[0:1, 0:c.QB],
                           wait=[("mm", mavT)])
                mzr = p.vg("reciprocal", zrow[0:1, 0:c.QB], zrow[0:1, 0:c.QB])
                mbc = p.mm(psB[:, 0:c.QB], cs["onesrow"], zrow[0:1, 0:c.QB],
                           start=True, stop=True, sem="mm",
                           wait=[("evv", mzr)])
                mrb = p.v("tensor_copy", recb, psB[:, 0:c.QB], sem="evv",
                          wait=[("mm", mbc)])
                p.bks("B", "evv", mrb)
                mavf = p.scratch.get("avflush", 0)
                mavev = p.v("scalar_tensor_tensor", avpan[:, 0:c.QB],
                            pav[:, 0:c.QB], 1.0, recb, op0=ALU.mult,
                            op1=ALU.mult, sem="evv",
                            wait=[("evv", mrb), ("sta", mavf)])
                p.bks(avkey, "evv", mavev)
                mfl = p.dma(
                    "sync",
                    t.attnT_d.ap()[h * 128:(h + 1) * 128,
                                   b * c.S + qq * c.QB:b * c.S + (qq + 1) * c.QB],
                    avpan[:, 0:c.QB], "sta", wait=[("evv", mavev)])
                p.scratch["avflush"] = mfl
        msta = p.mark("sta")
        for dc in range(c.DC):
            p.dma("gpsimd", woT_sb[:, dc * c.D:(dc + 1) * c.D],
                  t.woT.ap()[dc * 128:(dc + 1) * 128, :], "lwc",
                  wait=[("mm", mWfree[(b, c.H - 1)]),
                        ("evs", p.scratch.get("psArel", 0))] if dc == 0 else None)
        mwo = p.mark("lwc")
        for blk in range(c.S // c.SW):
            atbrel = p.scratch.get("atbrel", 0)
            for dc in range(c.DC):
                p.dma("sync", atb[:, dc * c.SW:(dc + 1) * c.SW],
                      t.attnT_d.ap()[dc * 128:(dc + 1) * 128,
                                     b * c.S + blk * c.SW:b * c.S + (blk + 1) * c.SW],
                      "ldx",
                      wait=[("sta", msta), ("mm", atbrel)] if dc == 0 else None)
            matb = p.mark("ldx")
            for fc in range(c.DC):
                ps = psC if fc % 2 == 0 else psD
                bkey = "C" if fc % 2 == 0 else "D"
                for dc in range(c.DC):
                    p.mm(ps[:, 0:c.SW],
                         woT_sb[:, dc * c.D + fc * 128:dc * c.D + (fc + 1) * 128],
                         atb[:, dc * c.SW:(dc + 1) * c.SW],
                         start=(dc == 0), stop=(dc == c.DC - 1),
                         sem="mm" if dc == c.DC - 1 else None,
                         wait=[("ldx", matb), ("lwc", mwo), p.bkw(bkey)]
                         if dc == 0 else None)
                mog = p.mark("mm")
                moev = p.v("tensor_copy", oevac, ps[:, 0:c.SW], sem="evv",
                           wait=[("mm", mog), ("sao", p.scratch.get("oevst", 0))])
                p.bks(bkey, "evv", moev)
                mst = p.dma(
                    "sync",
                    t.aoutdT.ap()[fc * 128:(fc + 1) * 128,
                                  b * c.S + blk * c.SW:b * c.S + (blk + 1) * c.SW],
                    oevac, "sao", wait=[("evv", moev)])
                p.scratch["oevst"] = mst
            p.scratch["atbrel"] = p.mark("mm")
        p.scratch["outproj_mm"] = p.mark("mm")
    mattn = p.mark("mm")

    mcc1 = p.mark("cc1")
    msao_a = p.mark("sao")
    nper = c.NT * c.E
    for r in range(c.NC):
        p.dma("sync", B[0][:, r * nper:(r + 1) * nper],
              t.lgout.ap()[r * 128:(r + 1) * 128, :], "ldx",
              wait=[("cc1", mcc1), ("mm", mattn), ("sao", msao_a)]
              if r == 0 else None)
    mlgld = p.mark("ldx")
    p.wait("vector", "ldx", mlgld)
    p.wait("scalar", "ldx", mlgld)
    p.wait("vector", "mm", mattn)
    p.wait("scalar", "mm", mattn)
    p.wait("vector", "sao", msao_a)
    p.wait("scalar", "sao", msao_a)
    p.sg("activation", B[1], B[0], AF.Abs)
    p.sg("activation", B[2], B[1], AF.Exp, scale=-1.0)
    mln1 = p.sg("activation", B[1], B[2], AF.Ln, bias=1.0)
    msp = p.v("scalar_tensor_tensor", B[2], B[0], 0.0, B[1], op0=ALU.max,
              op1=ALU.add, sem="evv", wait=[("evs", mln1)])
    msco = p.s("activation", B[3], B[2], AF.Sqrt, sem="evs", wait=[("evv", msp)])
    p.wait("vector", "evs", msco)
    gbb = cs["gateb"].unsqueeze(1).broadcast_to([128, c.JT, c.E])
    p.vg("tensor_tensor", v_je(B[0]), v_je(B[3]), gbb, ALU.add)
    p.vg("tensor_reduce", m1, v_je(B[0]), AX.X, ALU.max)
    m1b = m1.unsqueeze(1).broadcast_to([128, c.E, c.JT])
    p.vg("tensor_tensor", v_ej(B[1]), v_ej(B[0]), m1b, ALU.is_equal)
    ierb = cs["iotaE2"][:, c.E:2 * c.E].unsqueeze(1).broadcast_to(
        [128, c.JT, c.E])
    p.vg("scalar_tensor_tensor", v_je(B[1]), v_je(B[1]), 1.0e6, ierb,
        op0=ALU.mult, op1=ALU.add)
    p.vg("tensor_reduce", q1, v_je(B[1]), AX.X, ALU.max)
    p.vg("tensor_scalar", idx1, q1, -1.0, float(1.0e6 + c.E - 1),
         op0=ALU.mult, op1=ALU.add)
    q1b = q1.unsqueeze(1).broadcast_to([128, c.E, c.JT])
    p.vg("tensor_tensor", v_ej(B[4]), v_ej(B[1]), q1b, ALU.is_equal)
    p.vg("scalar_tensor_tensor", B[1], B[4], NEG, B[0], op0=ALU.mult, op1=ALU.add)
    p.vg("tensor_reduce", m2, v_je(B[1]), AX.X, ALU.max)
    m2b = m2.unsqueeze(1).broadcast_to([128, c.E, c.JT])
    p.vg("tensor_tensor", v_ej(B[2]), v_ej(B[1]), m2b, ALU.is_equal)
    p.vg("scalar_tensor_tensor", v_je(B[2]), v_je(B[2]), 1.0e6, ierb,
        op0=ALU.mult, op1=ALU.add)
    p.vg("tensor_reduce", q2, v_je(B[2]), AX.X, ALU.max)
    p.vg("tensor_scalar", idx2, q2, -1.0, float(1.0e6 + c.E - 1),
         op0=ALU.mult, op1=ALU.add)
    q2b = q2.unsqueeze(1).broadcast_to([128, c.E, c.JT])
    p.vg("tensor_tensor", v_ej(B[5]), v_ej(B[2]), q2b, ALU.is_equal)
    p.vg("tensor_add", B[6], B[4], B[5])
    mO = p.vg("tensor_copy", Obf, B[6])
    p.vg("tensor_mul", B[1], B[4], B[3])
    p.vg("tensor_reduce", w1, v_je(B[1]), AX.X, ALU.add)
    p.vg("tensor_mul", B[1], B[5], B[3])
    p.vg("tensor_reduce", w2, v_je(B[1]), AX.X, ALU.add)
    p.vg("tensor_add", wsum, w1, w2)
    p.vg("reciprocal", winv, wsum)
    p.vg("tensor_mul", wn1, w1, winv)
    p.vg("tensor_mul", wn2, w2, winv)
    nbat = 512 // c.E
    relBt = 0
    for jb in range(0, c.JT, nbat):
        nthis = min(nbat, c.JT - jb)
        ps = psC if (jb // nbat) % 2 == 0 else psD
        bkey = "C" if (jb // nbat) % 2 == 0 else "D"
        for k in range(nthis):
            j = jb + k
            p.mm(ps[:, k * c.E:(k + 1) * c.E], cs["tmat"],
                 Obf[:, j * c.E:(j + 1) * c.E], start=True, stop=True,
                 wait=[("evv", mO), p.bkw(bkey), p.bkw("B")] if k == 0 else None)
            p.mm(psB[0:1, k * c.E:(k + 1) * c.E], cs["onescol_b"],
                 Obf[:, j * c.E:(j + 1) * c.E], start=True, stop=True,
                 sem="mm" if k == nthis - 1 else None)
        mgrp = p.mark("mm")
        msie = p.v("tensor_copy", B[2][:, jb * c.E:(jb + nthis) * c.E],
                   ps[:, 0:nthis * c.E], sem="evv", wait=[("mm", mgrp)])
        p.bks(bkey, "evv", msie)
        relBt = p.v("tensor_copy", trow[0:1, 0:nthis * c.E],
                    psB[0:1, 0:nthis * c.E], sem="evv",
                    wait=[("sro", p.scratch.get("trowst", 0))])
        p.bks("B", "evv", relBt)
        p.scratch["trowst"] = p.dma(
            "sync", t.totd.ap()[jb * c.E:(jb + nthis) * c.E].unsqueeze(0),
            trow[0:1, 0:nthis * c.E], "sro", wait=[("evv", relBt)])
    mtot = p.mark("sro")
    nchk = (c.JT + 127) // 128
    assert nchk <= 2
    mtz = p.v("memset", totch, 0.0, sem="evv")
    for ch in range(nchk):
        rows = min(128, c.JT - ch * 128)
        p.dma("sync", totch[0:rows, ch * c.E:(ch + 1) * c.E],
              t.totd.ap()[ch * 128 * c.E:(ch * 128 + rows) * c.E]
              .rearrange("(r e) -> r e", e=c.E), "ldx",
              wait=[("sro", mtot), ("evv", mtz)] if ch == 0 else None)
    mtch = p.mark("ldx")
    for ch in range(nchk):
        lastmm = (ch == nchk - 1) and (nchk == 1)
        p.mm(psB[:, ch * c.E:(ch + 1) * c.E], cs["tmatf"][:, 0:128],
             totch[:, ch * c.E:(ch + 1) * c.E],
             start=True, stop=(ch == 0 or nchk == 1),
             sem="mm" if lastmm else None,
             wait=[("ldx", mtch), p.bkw("B")] if ch == 0 else None)
        if ch == 1:
            p.mm(psB[:, c.E:2 * c.E], cs["tmatf"][:, 128:256],
                 totch[:, 0:c.E], start=False, stop=True, sem="mm")
    mbmm = p.mark("mm")
    mbev = p.v("tensor_copy", basej[:, 0:nchk * c.E], psB[:, 0:nchk * c.E],
               sem="evv", wait=[("mm", mbmm)])
    p.bks("B", "evv", mbev)
    for ch in range(nchk):
        rows = min(128, c.JT - ch * 128)
        p.dma("sync", t.based.ap()[ch * 128 * c.E:(ch * 128 + rows) * c.E]
              .rearrange("(r e) -> r e", e=c.E),
              basej[0:rows, ch * c.E:(ch + 1) * c.E], "sro",
              wait=[("evv", mbev)])
    mbst = p.mark("sro")
    if JTE <= 512:
        baserow = trow[0:1, 0:JTE]
    else:
        baserow = B[1][0:1, 0:JTE]
    mrow = p.dma("sync", baserow, t.based.ap().unsqueeze(0), "ldx",
                 wait=[("sro", mbst)])
    nbc = (JTE + 511) // 512
    for nb in range(nbc):
        lo, hi = nb * 512, min(JTE, (nb + 1) * 512)
        ps = psC if nb % 2 == 0 else psD
        bkey = "C" if nb % 2 == 0 else "D"
        p.mm(ps[:, 0:hi - lo], cs["onesrow"], baserow[0:1, lo:hi],
             start=True, stop=True, sem="mm",
             wait=[("ldx", mrow), p.bkw(bkey)])
        mg = p.mark("mm")
        p.vg("tensor_sub", B[2][:, lo:hi], B[2][:, lo:hi], B[6][:, lo:hi],
             wait=[("mm", mg)])
        mz = p.vg("tensor_add", B[2][:, lo:hi], B[2][:, lo:hi],
                  ps[:, 0:hi - lo])
        p.bks(bkey, "evv", mz)
    p.vg("tensor_mul", B[1], B[4], B[2])
    p.vg("tensor_reduce", pos1, v_je(B[1]), AX.X, ALU.add)
    p.vg("tensor_mul", B[1], B[5], B[2])
    p.vg("tensor_reduce", pos2, v_je(B[1]), AX.X, ALU.add)
    selAb = cs["selAB"][:, 0:c.E].unsqueeze(1).broadcast_to([128, c.JT, c.E])
    selBb = cs["selAB"][:, c.E:2 * c.E].unsqueeze(1).broadcast_to(
        [128, c.JT, c.E])
    p.vg("tensor_tensor", v_je(B[1]), v_je(B[6]), selAb, ALU.mult)
    p.vg("tensor_reduce", mAv, v_je(B[1]), AX.X, ALU.add)
    p.vg("tensor_tensor", v_je(B[1]), v_je(B[2]), selAb, ALU.mult)
    p.vg("tensor_reduce", pAv, v_je(B[1]), AX.X, ALU.add)
    p.vg("tensor_tensor", v_je(B[1]), v_je(B[6]), selBb, ALU.mult)
    p.vg("tensor_reduce", mBv, v_je(B[1]), AX.X, ALU.add)
    p.vg("tensor_tensor", v_je(B[1]), v_je(B[2]), selBb, ALU.mult)
    p.vg("tensor_reduce", pBv, v_je(B[1]), AX.X, ALU.add)
    p.vg("tensor_scalar", val1, pos1, float(c.CAP), None, op0=ALU.is_lt)
    p.vg("tensor_scalar", val2, pos2, float(c.CAP), None, op0=ALU.is_lt)
    p.vg("tensor_mul", wn1, wn1, val1)
    p.vg("tensor_mul", wn2, wn2, val2)
    for (posx, idxx, valx, rix) in ((pos1, idx1, val1, ri1),
                                    (pos2, idx2, val2, ri2)):
        p.vg("tensor_copy", it1, posx)
        p.vg("tensor_scalar", it2, it1, c.A - 1, None, op0=ALU.bitwise_and)
        p.vg("tensor_scalar", it2, it2, 7, None, op0=ALU.logical_shift_left)
        p.vg("tensor_scalar", it3, it1, c.LA, None,
             op0=ALU.logical_shift_right)
        p.vg("tensor_add", it2, it2, it3)
        p.vg("tensor_scalar", it3, it2, c.LCW, None,
             op0=ALU.logical_shift_right)
        p.vg("tensor_scalar", it5, it2, c.CW - 1, None,
             op0=ALU.bitwise_and)
        p.vg("tensor_copy", it1, idxx)
        p.vg("tensor_scalar", it6, it1, c.LNC, None,
             op0=ALU.logical_shift_right)
        p.vg("tensor_scalar", it1, it1, c.NC - 1, None,
             op0=ALU.bitwise_and)
        p.vg("tensor_scalar", it6, it6, c.LNCB, None,
             op0=ALU.logical_shift_left)
        p.vg("tensor_add", it6, it6, it3)
        p.vg("tensor_scalar", it6, it6, c.LNC, None,
             op0=ALU.logical_shift_left)
        p.vg("tensor_add", it6, it6, it1)
        p.vg("tensor_scalar", it6, it6, c.LCW, None,
             op0=ALU.logical_shift_left)
        p.vg("tensor_add", it2, it6, it5)
        p.vg("tensor_copy", it4, valx)
        p.vg("tensor_tensor", rix, it2, it4, ALU.mult)
    if t.dbgrt is not None:
        dlist = [m1, m2, idx1, idx2, w1, w2, wn1, wn2, pos1, pos2, val1,
                 val2, mAv, pAv, mBv, pBv, wsum, winv]
        mduw = p.v("tensor_copy", it1, it1, sem="evv")
        for kk, sv in enumerate(dlist):
            p.dma("sync", t.dbgrt.ap()[:, kk * c.JT:(kk + 1) * c.JT], sv,
                  "sta", wait=[("evv", mduw)] if kk == 0 else None)
        p.dma("sync", t.dbgsco.ap(), B[3], "sta")
        p.dma("sync", t.dbgsg.ap(), B[2], "sta")
    mwv = p.vg("tensor_copy", it4, it3)
    p.dma("sync", t.r1d.ap(), ri1, "sro", wait=[("evv", mwv)])
    p.dma("sync", t.r2d.ap(), ri2, "sro")
    p.dma("sync", t.wv1d.ap(), wn1, "sro")
    mrw = p.dma("sync", t.wv2d.ap(), wn2, "sro")
    if p.cur == "gpsimd":
        coff = nc.values_load(t.cmeta_sb.ap()[0:1, 0:1],
                              engines=[mybir.EngineType.Pool],
                              min_val=0, max_val=c.NC - 1,
                              skip_runtime_bounds_check=True)
    else:
        coff = 0
    p.dma("gpsimd", t.r1l.ap(), t.r1d.ap()[:, bass.ts(coff, c.NT)], "gat",
          wait=[("sro", mrw)])
    p.dma("gpsimd", t.r2l.ap(), t.r2d.ap()[:, bass.ts(coff, c.NT)], "gat")
    p.dma("gpsimd", t.wv1l.ap(), t.wv1d.ap()[:, bass.ts(coff, c.NT)], "gat")
    mloc = p.dma("gpsimd", t.wv2l.ap(), t.wv2d.ap()[:, bass.ts(coff, c.NT)],
                 "gat")
    mroute_dve = p.mark("evv")

    nhalf = (c.CAP + 511) // 512
    pbrel = {}
    for j in range(c.JT):
        for e2 in range(2):
            pb = (2 * j + e2) % 4
            mX, pX = (mAv, pAv) if e2 == 0 else (mBv, pBv)
            mpb = p.v("tensor_scalar", Pb[pb], cs["iotaCAP"], pX[:, j:j + 1],
                      mX[:, j:j + 1], op0=ALU.is_equal, op1=ALU.mult, sem="evv",
                      wait=[("mm", pbrel.get(pb, 0))])
            for hf in range(nhalf):
                lo = hf * 512
                hi = min(c.CAP, lo + 512)
                w = [("evv", mpb)] if hf == 0 else []
                if j == 0:
                    w = list(w) + [p.bkw("A%d" % qd) for qd in range(4)]
                p.mm(psA[0:2, e2 * c.EOFF + lo:e2 * c.EOFF + hi],
                     cs["cmpT"][:, 2 * j:2 * j + 2], Pb[pb][:, lo:hi],
                     start=(j == 0), stop=(j == c.JT - 1),
                     sem="mm", wait=w)
            pbrel[pb] = p.mark("mm")
    mcmp = p.mark("mm")
    for e2 in range(2):
        mev = p.v("tensor_copy", sb2[0:2, 0:c.CAP],
                  psA[0:2, e2 * c.EOFF:e2 * c.EOFF + c.CAP], sem="evv",
                  wait=[("mm", mcmp), ("sto", p.scratch.get("tokst", 0))])
        for qd in range(4):
            p.bks("A%d" % qd, "evv", mev)
        for hf in range(nhalf):
            lo = hf * 512
            hi = min(c.CAP, lo + 512)
            ps = psB if hf == 0 else psC
            bkey = "B" if hf == 0 else "C"
            p.mm(ps[0:1, 0:hi - lo], cs["coef2"], sb2[0:2, lo:hi],
                 start=True, stop=True, sem="mm",
                 wait=[("evv", mev), p.bkw(bkey)])
            mg = p.mark("mm")
            mtr_ = p.vg("tensor_copy", tokrow[0:1, lo:hi], ps[0:1, 0:hi - lo],
                        wait=[("mm", mg)])
            p.bks(bkey, "evv", mtr_)
        mtk = p.vg("tensor_copy", tokrowi[0:1, 0:c.CAP],
                   tokrow[0:1, 0:c.CAP])
        mts = p.dma("sync", t.tokd.ap()[e2, :].unsqueeze(0),
                    tokrowi[0:1, 0:c.CAP], "sto", wait=[("evv", mtk)])
        p.scratch["tokst"] = mts
        p.dma("gpsimd", (t.tokA_sb if e2 == 0 else t.tokB_sb).ap(),
              t.tokd.ap()[e2, :].rearrange("(q a) -> q a", a=c.A), "gtk",
              wait=[("sto", mts)])
    mtok = p.mark("gtk")
    mtok_sta = p.mark("sto")

    for dc in range(c.DC):
        p.dma("gpsimd", ws1_sb[:, dc * c.I:(dc + 1) * c.I],
              t.ws1T.ap()[dc * 128:(dc + 1) * 128, :], "lwc",
              wait=[("evv", mroute_dve), ("sto", mtok_sta),
                    ("mm", mcmp)] if dc == 0 else None)
    for dc in range(c.DC):
        p.dma("gpsimd", ws3_sb[:, dc * c.I:(dc + 1) * c.I],
              t.ws3T.ap()[dc * 128:(dc + 1) * 128, :], "lwc")
    for ic in range(c.IC):
        p.dma("gpsimd", ws2_sb[:, ic * c.D:(ic + 1) * c.D],
              t.ws2T.ap()[ic * 128:(ic + 1) * 128, :], "lwc")
    mws = p.mark("lwc")
    mws2 = mws
    for blk in range(c.NBLK):
        xb = xnTblk[blk % 2]
        prev_use = p.scratch.get("xnblk1", 0)
        for dc in range(c.DC):
            p.dma("sync", xb[:, dc * c.NB:(dc + 1) * c.NB],
                  t.xnT_d.ap()[dc * 128:(dc + 1) * 128,
                               blk * c.NB:(blk + 1) * c.NB], "ldx",
                  wait=[("mm", max(prev_use, mcmp)), ("evv", mroute_dve),
                        ("stp0", mstp0), ("stp1", mstp1), ("sto", mtok_sta)]
                  if dc == 0 else None)
        mxb = p.mark("ldx")
        nsb = max(c.NB // 512, 1)
        wsb_ = min(512, c.NB)
        for ic in range(c.IC):
            for ns2 in range(nsb):
                for dc in range(c.DC):
                    p.mm(psC[:, 0:wsb_],
                         ws1_sb[:, dc * c.I + ic * 128:dc * c.I + (ic + 1) * 128],
                         xb[:, dc * c.NB + ns2 * wsb_:dc * c.NB + (ns2 + 1) * wsb_],
                         start=(dc == 0), stop=(dc == c.DC - 1),
                         sem="mm" if dc == c.DC - 1 else None,
                         wait=[("lwc", mws), ("ldx", mxb), p.bkw("C")]
                         if dc == 0 else None)
                mg_ = p.mark("mm")
                for dc in range(c.DC):
                    p.mm(psD[:, 0:wsb_],
                         ws3_sb[:, dc * c.I + ic * 128:dc * c.I + (ic + 1) * 128],
                         xb[:, dc * c.NB + ns2 * wsb_:dc * c.NB + (ns2 + 1) * wsb_],
                         start=(dc == 0), stop=(dc == c.DC - 1),
                         sem="mm" if dc == c.DC - 1 else None,
                         wait=[p.bkw("D")] if dc == 0 else None)
                mu_ = p.mark("mm")
                ms = p.s("activation", silg[:, 0:wsb_], psC[:, 0:wsb_], AF.Silu,
                         sem="evs",
                         wait=[("mm", mg_), ("evv", p.scratch.get("shsil", 0))])
                p.bks("C", "evs", ms)
                mh = p.v("tensor_tensor",
                         hTblk[:, ic * c.NB + ns2 * wsb_:ic * c.NB + (ns2 + 1) * wsb_],
                         silg[:, 0:wsb_], psD[:, 0:wsb_], ALU.mult, sem="evv",
                         wait=[("mm", mu_), ("evs", ms)])
                p.scratch["shsil"] = mh
                p.bks("D", "evv", mh)
        mhdone = p.mark("evv")
        p.scratch["xnblk1"] = p.mark("mm")
        for fc in range(c.DC):
            for ns2 in range(nsb):
                ps = psC if fc % 2 == 0 else psD
                bkey = "C" if fc % 2 == 0 else "D"
                for ic in range(c.IC):
                    p.mm(ps[:, 0:wsb_],
                         ws2_sb[:, ic * c.D + fc * 128:ic * c.D + (fc + 1) * 128],
                         hTblk[:, ic * c.NB + ns2 * wsb_:ic * c.NB + (ns2 + 1) * wsb_],
                         start=(ic == 0), stop=(ic == c.IC - 1),
                         sem="mm" if ic == c.IC - 1 else None,
                         wait=[("lwc", mws2), ("evv", mhdone), p.bkw(bkey),
                               ("ssh", p.scratch.get("shst", 0))]
                         if ic == 0 else None)
                mo_ = p.mark("mm")
                mev = p.v("tensor_copy", shev[:, 0:wsb_], ps[:, 0:wsb_],
                          sem="evv", wait=[("mm", mo_)])
                p.bks(bkey, "evv", mev)
                mshst = p.dma(
                    "sync",
                    t.shoutdT.ap()[fc * 128:(fc + 1) * 128,
                                   blk * c.NB + ns2 * wsb_:blk * c.NB + (ns2 + 1) * wsb_],
                    shev[:, 0:wsb_], "ssh", wait=[("evv", mev)])
                p.scratch["shst"] = mshst
    mshared = p.mark("mm")

    mcc2 = p.mark("cc2")
    mssh_e = p.mark("ssh")
    for e2 in range(2):
        tok_sb = t.tokA_sb if e2 == 0 else t.tokB_sb
        for a in range(c.A):
            par = a % 2
            gas = "ga0" if par == 0 else "ga1"
            relg = p.scratch.get(("gxrel", par), mshared)
            mz = p.v("memset", gx[par], 0.0, sem="evv",
                     wait=[("mm", relg), ("ssh", mssh_e)])
            mgat = p.g("indirect_dma_start", out=gx[par], out_offset=None,
                       in_=t.xnout.ap(),
                       in_offset=IndirectOffsetOnAxis(tok_sb.ap()[:, a:a + 1], 0),
                       sem=gas,
                       wait=[("evv", mz), ("cc2", mcc2), ("gtk", mtok)])
            for gtr in range(0, c.DC, 8):
                n8 = min(8, c.DC - gtr)
                for k8 in range(n8):
                    dc = gtr + k8
                    p.tr(psE[:, k8 * 128:(k8 + 1) * 128],
                         gx[par][:, dc * 128:(dc + 1) * 128], cs["identb"],
                         sem="mm" if k8 == n8 - 1 else None,
                         wait=[(gas, mgat), p.bkw("E"), ("mm", mshared)]
                         if k8 == 0 else None)
                mtr8 = p.mark("mm")
                dst = bufT3[:, gtr:gtr + n8, a * 128:(a + 1) * 128]
                mevE = p.v("tensor_copy", dst, psE[:, 0:n8 * 128], sem="evv",
                           wait=[("mm", mtr8)])
                p.bks("E", "evv", mevE)
            p.scratch[("gxrel", par)] = p.mark("mm")
        mbufT = p.mark("evv")
        ncb = max(c.CAP // c.CW, 1)
        for ic in range(c.IC):
            par = ic % 2
            lwe = "lwa" if par == 0 else "lwb"
            wrel = p.scratch.get(("w13rel", par), mshared)
            mw13 = p.dma("gpsimd", w13_b[par], t.w13p.ap()[e2, ic], lwe,
                         wait=[("mm", wrel), ("ssh", mssh_e)])
            for ns2 in range(ncb):
                for dc in range(c.DC):
                    p.mm(psC[:, 0:c.CW], w13_b[par][:, dc * 128:(dc + 1) * 128],
                         bufT[:, dc * c.CAP + ns2 * c.CW:dc * c.CAP + (ns2 + 1) * c.CW],
                         start=(dc == 0), stop=(dc == c.DC - 1),
                         sem="mm" if dc == c.DC - 1 else None,
                         wait=[(lwe, mw13), ("evv", mbufT), p.bkw("C")]
                         if dc == 0 else None)
                mg_ = p.mark("mm")
                for dc in range(c.DC):
                    p.mm(psD[:, 0:c.CW],
                         w13_b[par][:, c.DC * 128 + dc * 128:c.DC * 128 + (dc + 1) * 128],
                         bufT[:, dc * c.CAP + ns2 * c.CW:dc * c.CAP + (ns2 + 1) * c.CW],
                         start=(dc == 0), stop=(dc == c.DC - 1),
                         sem="mm" if dc == c.DC - 1 else None,
                         wait=[p.bkw("D")] if dc == 0 else None)
                mu_ = p.mark("mm")
                ms = p.s("activation", sile[:, 0:c.CW], psC[:, 0:c.CW], AF.Silu,
                         sem="evs",
                         wait=[("mm", mg_), ("evv", p.scratch.get("esil", 0))])
                p.bks("C", "evs", ms)
                mh = p.v("tensor_tensor",
                         hTe[:, ic * c.CAP + ns2 * c.CW:ic * c.CAP + (ns2 + 1) * c.CW],
                         sile[:, 0:c.CW], psD[:, 0:c.CW], ALU.mult, sem="evv",
                         wait=[("mm", mu_), ("evs", ms)])
                p.scratch["esil"] = mh
                p.bks("D", "evv", mh)
            p.scratch[("w13rel", par)] = p.mark("mm")
        mhe = p.mark("evv")
        for ns2 in range(ncb):
            for fc in range(c.DC):
                par = fc % 2
                lwe2 = "lw2a" if par == 0 else "lw2b"
                wrel = p.scratch.get(("ew2rel", par), mshared)
                mw2 = p.dma("gpsimd", w2_b[par], t.w2q.ap()[e2, fc], lwe2,
                            wait=[("mm", wrel), ("ssh", mssh_e)])
                ps = psC if fc % 2 == 0 else psD
                bkey = "C" if fc % 2 == 0 else "D"
                for ic in range(c.IC):
                    p.mm(ps[:, 0:c.CW], w2_b[par][:, ic * 128:(ic + 1) * 128],
                         hTe[:, ic * c.CAP + ns2 * c.CW:ic * c.CAP + (ns2 + 1) * c.CW],
                         start=(ic == 0), stop=(ic == c.IC - 1),
                         sem="mm" if ic == c.IC - 1 else None,
                         wait=[(lwe2, mw2), ("evv", mhe), p.bkw(bkey)]
                         if ic == 0 else None)
                mo_ = p.mark("mm")
                p.scratch[("ew2rel", par)] = mo_
                mev = p.v("tensor_copy", eev[:, 0:c.CW], ps[:, 0:c.CW],
                          sem="evv",
                          wait=[("mm", mo_), ("trp", p.scratch.get("etrp", 0))])
                p.bks(bkey, "evv", mev)
                nsub = c.CW // 128
                for sub in range(nsub):
                    p.dma("scalar",
                          epan[:, sub * c.D + fc * 128:sub * c.D + (fc + 1) * 128],
                          eev[:, sub * 128:(sub + 1) * 128], "trp",
                          transpose=True,
                          wait=[("evv", mev),
                                ("out", p.scratch.get("epanst", 0))]
                          if sub == 0 else None)
                p.scratch["etrp"] = p.mark("trp")
            metrp = p.mark("trp")
            nsub = c.CW // 128
            for sub in range(nsub):
                row0 = e2 * c.CAP + ns2 * c.CW + sub * 128
                p.dma("sync", t.outin.ap()[row0:row0 + 128, :],
                      epan[:, sub * c.D:(sub + 1) * c.D], "out",
                      wait=[("trp", metrp)] if sub == 0 else None)
            p.scratch["epanst"] = p.mark("out")
            qr = e2 * c.NCB + ns2
            p.g("collective_compute", "AllGather", ALU.bypass,
                replica_groups=[list(range(c.NC))],
                ins=[t.outin.ap()[e2 * c.CAP + ns2 * c.CW:
                                  e2 * c.CAP + (ns2 + 1) * c.CW, :].opt()],
                outs=[t.outout.ap()[qr * c.NC * c.CW:
                                    (qr + 1) * c.NC * c.CW, :].opt()],
                sem="cc3", n=1, wait=[("out", p.mark("out"))])
    mcc3 = p.mark("cc3")
    mout_all = p.mark("out")

    msao = p.mark("sao")
    mssh = p.mark("ssh")
    for jj in range(c.NT):
        relA = p.scratch.get("cmbA", 0)
        mx = p.dma("sync", xt2, t.x_loc.ap()[jj * 128:(jj + 1) * 128, :], "ldx",
                   wait=[("evv", relA)])
        hD = c.D // 2
        for hh in range(2):
            p.dma("sync", aot[:, hh * hD:(hh + 1) * hD],
                  t.aoutdT.ap()[hh * hD:(hh + 1) * hD, jj * 128:(jj + 1) * 128],
                  "trp", transpose=True,
                  wait=[("sao", msao), ("ssh", mssh)] if (jj == 0 and hh == 0)
                  else None)
            p.dma("sync", sht[:, hh * hD:(hh + 1) * hD],
                  t.shoutdT.ap()[hh * hD:(hh + 1) * hD, jj * 128:(jj + 1) * 128],
                  "trp", transpose=True)
        mtl = p.mark("trp")
        p.wait("vector", "ldx", mx)
        p.wait("vector", "trp", mtl)
        p.wait("vector", "oA", p.scratch.get("cmbAst", 0))
        p.vg("tensor_add", acc, xt2, aot)
        mrel0 = p.vg("tensor_add", acc, acc, sht)
        p.scratch["cmbA"] = mrel0
        mcst = p.dma("sync", t.out_loc.ap()[jj * 128:(jj + 1) * 128, :], acc,
                     "oA", wait=[("evv", mrel0)])
        p.scratch["cmbAst"] = mcst
    mAdone = p.mark("oA")
    mexp_done = p.mark("mm")
    for jj in range(c.NT):
        k = jj % NSET
        g0, g1, xt2b, yv = g0s[k], g1s[k], xt2s[k], yvs[k]
        gbk, lok, obk = "gb%d" % k, "lo%d" % k, "ob%d" % k
        relg = p.scratch.get(("cmbBg", k), 0)
        p.g("indirect_dma_start", out=g0, out_offset=None,
            in_=t.outout.ap(),
            in_offset=IndirectOffsetOnAxis(t.r1l.ap()[:, jj:jj + 1], 0),
            sem=gbk,
            wait=[("cc3", mcc3), ("gat", mloc), ("mm", mexp_done),
                  ("out", mout_all)]
            if jj == 0 else ([("evv", relg)] if jj >= NSET else None))
        p.g("indirect_dma_start", out=g1, out_offset=None,
            in_=t.outout.ap(),
            in_offset=IndirectOffsetOnAxis(t.r2l.ap()[:, jj:jj + 1], 0),
            sem=gbk)
        mgb = p.mark(gbk)
        rell = p.scratch.get(("cmbBl", k), 0)
        p.dma("sync", xt2b, t.out_loc.ap()[jj * 128:(jj + 1) * 128, :], lok,
              wait=[("oA", mAdone), ("out", mout_all)] if jj == 0 else
                   ([("evv", rell)] if jj >= NSET else None))
        mld = p.mark(lok)
        p.vg("tensor_scalar_mul", yv, g0, t.wv1l.ap()[:, jj:jj + 1],
             wait=[(gbk, mgb), (obk, p.scratch.get(("cmbBst", k), 0))])
        mrelg = p.vg("scalar_tensor_tensor", yv, g1,
                     t.wv2l.ap()[:, jj:jj + 1], yv,
                     op0=ALU.mult, op1=ALU.add)
        p.scratch[("cmbBg", k)] = mrelg
        mrel0 = p.vg("tensor_add", yv, yv, xt2b, wait=[(lok, mld)])
        p.scratch[("cmbBl", k)] = mrel0
        mcst = p.dma("scalar", t.out_loc.ap()[jj * 128:(jj + 1) * 128, :],
                     yv, obk, wait=[("evv", mrel0)])
        p.scratch[("cmbBst", k)] = mcst
    p.wait("sync", "oA", p.mark("oA"))
    for k in range(NSET):
        p.wait("sync", "ob%d" % k, p.mark("ob%d" % k))
    mout = p.mark("out")
    for name, (src, dstt) in t.dbg.items():
        p.dma("gpsimd", dstt.ap(), src.ap(), "gat", wait=[("out", mout)])
    mdbg = p.mark("gat")
    p.wait("gpsimd", "gat", mdbg)
    p.wait("sync", "out", mout)



def build(cfg, dbg=()):
    c = cfg
    nc = bass.Bass("TRN2", target_bir_lowering=False)
    t = NS()
    t.ARENA = 88064 if c.D >= 2048 else (40960 if c.S >= 1024 else 26624)

    def ein(name, shape, dtype=F32):
        return nc.dram_tensor(name, list(shape), dtype, kind="ExternalInput")
    t.x_loc = ein("x_loc", (c.TL, c.D))
    t.wqkvp = ein("wqkvp", (c.H, 3, 128, c.D), BF16)
    t.woT = ein("woT", (c.D, c.D), BF16)
    t.ws1T = ein("ws1T", (c.D, c.I), BF16)
    t.ws3T = ein("ws3T", (c.D, c.I), BF16)
    t.ws2T = ein("ws2T", (c.I, c.D), BF16)
    t.w13p = ein("w13p", (2, c.IC, 128, 2 * c.D), BF16)
    t.w2q = ein("w2q", (2, c.DC, 128, c.I), BF16)
    t.cmeta = ein("cmeta", (1, 8), I32)

    const_shapes = {
        "identf": ((128, 128), F32), "identb": ((128, 128), BF16),
        "tmat": ((128, 128), BF16), "tmatf": ((128, 256), F32),
        "iotaCAP": ((128, c.CAP), F32), "iotaE2": ((128, 2 * c.E), F32),
        "cmpT": ((128, 2 * c.JT), BF16), "maskd": ((128, 128), F32),
        "gwT": ((128, c.DC * c.E), F32), "gateb": ((128, c.E), F32),
        "selAB": ((128, 2 * c.E), F32), "onesrow": ((1, 128), F32),
        "onescol_b": ((128, 1), BF16), "coef2": ((2, 1), F32),
        "actc": ((128, 8), F32),
        "maskT4b": ((128, c.QB * c.KCB), BF16),
        "onesrow_b": ((1, 128), BF16),
        "ones2b": ((2, 128), BF16),
    }
    t.const_names = list(const_shapes.keys())
    t.cdr = {}
    for name, (shp, dt) in const_shapes.items():
        t.cdr[name] = ein("c_" + name, shp, dt)

    t.xnT_d = nc.dram_tensor("xnT_d", [c.D, c.TL], BF16)
    t.xnin = nc.dram_tensor("xnin", [c.TL, c.D], BF16)
    t.xnout = nc.dram_tensor("xnout", [c.T, c.D], BF16)
    t.lgin = nc.dram_tensor("lgin", [128, c.NT * c.E], F32)
    t.lgout = nc.dram_tensor("lgout", [c.NC * 128, c.NT * c.E], F32)
    t.attnT_d = nc.dram_tensor("attnT_d", [c.D, c.TL], BF16)
    t.aoutdT = nc.dram_tensor("aoutdT", [c.D, c.TL], BF16)
    t.shoutdT = nc.dram_tensor("shoutdT", [c.D, c.TL], BF16)
    t.totd = nc.dram_tensor("totd", [c.JTE], F32)
    t.based = nc.dram_tensor("based", [c.JTE], F32)
    t.r1d = nc.dram_tensor("r1d", [128, c.JT], I32)
    t.r2d = nc.dram_tensor("r2d", [128, c.JT], I32)
    t.wv1d = nc.dram_tensor("wv1d", [128, c.JT], F32)
    t.wv2d = nc.dram_tensor("wv2d", [128, c.JT], F32)
    t.tokd = nc.dram_tensor("tokd", [2, c.CAP], I32)
    t.outin = nc.dram_tensor("outin", [2 * c.CAP, c.D], BF16)
    t.outout = nc.dram_tensor("outout", [c.E * c.CAP, c.D], BF16)
    t.out_loc = nc.dram_tensor("out_loc", [c.TL, c.D], F32,
                               kind="ExternalOutput")
    t.dbg = {}
    t.dbgst = None
    t.dbgrt = None
    t.dbgsco = None
    t.dbgsg = None
    if "ST" in dbg:
        dbg = tuple(d for d in dbg if d != "ST")
        t.dbgst = nc.dram_tensor("dbg_st", [128, c.NT * 16], F32,
                                 kind="ExternalOutput")
    if "RT" in dbg:
        dbg = tuple(d for d in dbg if d != "RT")
        t.dbgrt = nc.dram_tensor("dbg_rt", [128, 18 * c.JT], F32,
                                 kind="ExternalOutput")
        t.dbgsco = nc.dram_tensor("dbg_sco", [128, c.JTE], F32,
                                  kind="ExternalOutput")
        t.dbgsg = nc.dram_tensor("dbg_sg", [128, c.JTE], F32,
                                 kind="ExternalOutput")
    for name in dbg:
        src = getattr(t, name)
        t.dbg[name] = (src, nc.dram_tensor(
            "dbg_" + name, list(src.ap().shape), src.ap().dtype,
            kind="ExternalOutput"))

    sem_names = ["ldc", "ldx", "ldw", "mm", "evv", "evs", "trp", "gat",
                 "stp0", "sta", "out", "cc1", "cc2", "cc3",
                 "lx0", "lx1", "lwa", "lwb", "lwc", "sxn", "sxn1", "ga0", "ga1",
                 "sao", "ssh", "sro", "sto", "lw2a", "lw2b", "gtk", "stp1",
                 "oA", "oB", "gb0", "gb1", "gb2", "gb3",
                 "lo0", "lo1", "lo2", "lo3", "ob0", "ob1", "ob2", "ob3"]

    import contextlib
    with contextlib.ExitStack() as ctx:
        t.AR = ctx.enter_context(nc.sbuf_tensor("AR", [128, t.ARENA], BF16))
        t.csb = {}
        for name, (shp, dt) in const_shapes.items():
            hnd = ctx.enter_context(nc.sbuf_tensor("cs_" + name, list(shp), dt))
            t.csb[name] = hnd.ap()
        t.cmeta_sb = ctx.enter_context(nc.sbuf_tensor("cmeta_sb", [1, 8], I32))
        actvals = [1e-5, -1.0, 1.0, c.SCALE]
        for vi, vv in enumerate(actvals):
            nc.const_aps.register(F32, vv, t.csb["actc"][:, vi:vi + 1])
        t.r1l = ctx.enter_context(nc.sbuf_tensor("r1l", [128, c.NT], I32))
        t.r2l = ctx.enter_context(nc.sbuf_tensor("r2l", [128, c.NT], I32))
        t.wv1l = ctx.enter_context(nc.sbuf_tensor("wv1l", [128, c.NT], F32))
        t.wv2l = ctx.enter_context(nc.sbuf_tensor("wv2l", [128, c.NT], F32))
        t.tokA_sb = ctx.enter_context(nc.sbuf_tensor("tokA_sb", [128, c.A], I32))
        t.tokB_sb = ctx.enter_context(nc.sbuf_tensor("tokB_sb", [128, c.A], I32))
        t.psA = ctx.enter_context(nc.psum_tensor("psA", [128, 2048], F32))
        t.psB = ctx.enter_context(nc.psum_tensor("psB", [128, 512], F32))
        t.psC = ctx.enter_context(nc.psum_tensor("psC", [128, 512], F32))
        t.psD = ctx.enter_context(nc.psum_tensor("psD", [128, 512], F32))
        t.psE = ctx.enter_context(nc.psum_tensor("psE", [128, 1024], BF16))
        t.sems = {}
        for sn in sem_names:
            t.sems[sn] = ctx.enter_context(nc.semaphore(sn))
        with nc.Block() as block:
            @block.sync
            def _(eng):
                schedule(P(nc, c, t, "sync", {"sync": eng}))

            @block.vector
            def _(eng):
                schedule(P(nc, c, t, "vector", {"vector": eng}))

            @block.scalar
            def _(eng):
                schedule(P(nc, c, t, "scalar", {"scalar": eng}))

            @block.tensor
            def _(eng):
                schedule(P(nc, c, t, "tensor", {"tensor": eng}))

            @block.gpsimd
            def _(eng):
                schedule(P(nc, c, t, "gpsimd", {"gpsimd": eng}))
    return nc


def make_consts(c):
    r = np.arange(128, dtype=np.float32)
    tincl = (r[:, None] <= r[None, :]).astype(np.float32)
    tstrict = (r[:, None] < r[None, :]).astype(np.float32)
    tmatf = np.concatenate([tstrict, np.ones((128, 128), np.float32)], axis=1)
    cmpT = np.zeros((128, 2 * c.JT), np.float32)
    for j in range(c.JT):
        cmpT[:, 2 * j] = j
        cmpT[:, 2 * j + 1] = r
    maskd = np.where(r[None, :] <= r[:, None], 0.0, NEG).astype(np.float32)
    QB = c.QB
    KCB = c.KCB
    kp = np.arange(128)
    qf = np.arange(QB)
    mT4 = np.zeros((128, QB * KCB), np.float32)
    for d in range(KCB):
        mT4[:, d * QB:(d + 1) * QB] = (qf[None, :] >= kp[:, None] + d * 128)
    iotaE2 = np.concatenate([
        np.tile(np.arange(c.E, dtype=np.float32), (128, 1)),
        np.tile(np.arange(c.E - 1, -1, -1, dtype=np.float32), (128, 1))],
        axis=1)
    return {
        "identf": np.eye(128, dtype=np.float32),
        "identb": np.eye(128, dtype=np.float32).astype(ml_dtypes.bfloat16),
        "tmat": tincl.astype(ml_dtypes.bfloat16),
        "tmatf": tmatf,
        "iotaCAP": np.tile(np.arange(c.CAP, dtype=np.float32), (128, 1)),
        "iotaE2": iotaE2,
        "cmpT": cmpT.astype(ml_dtypes.bfloat16),
        "maskd": maskd,
        "onesrow": np.ones((1, 128), np.float32),
        "onescol_b": np.ones((128, 1), np.float32).astype(ml_dtypes.bfloat16),
        "coef2": np.array([[128.0], [1.0]], np.float32),
        "actc": np.tile(np.array([[1e-5, -1.0, 1.0, 1.0 / math.sqrt(128.0),
                           0.0, 0.0, 0.0, 0.0]], np.float32), (128, 1)),
        "maskT4b": mT4.astype(ml_dtypes.bfloat16),
        "onesrow_b": np.ones((1, 128), np.float32).astype(ml_dtypes.bfloat16),
        "ones2b": np.ones((2, 128), np.float32).astype(ml_dtypes.bfloat16),
    }


def make_inmaps(c, inputs):
    x = np.asarray(inputs["L0_attn_x"], np.float32).reshape(c.T, c.D)
    gw = np.asarray(inputs["L0_ffn_gate_w"], np.float32)
    gb = np.asarray(inputs["L0_ffn_gate_b"], np.float32)
    W1 = np.asarray(inputs["L0_ffn_W1"], np.float32)
    W3 = np.asarray(inputs["L0_ffn_W3"], np.float32)
    W2 = np.asarray(inputs["L0_ffn_W2"], np.float32)

    consts = make_consts(c)
    gwT_packed = np.ascontiguousarray(gw.T).reshape(c.DC, 128, c.E) \
        .transpose(1, 0, 2).reshape(128, c.DC * c.E).copy()
    shared = {
        "wqkvp": np.ascontiguousarray(np.stack([
            np.asarray(inputs["L0_attn_" + nm], np.float32)
            .reshape(c.H, 128, c.DC, 128).transpose(0, 3, 2, 1)
            .reshape(c.H, 128, c.D)
            for nm in ("wq", "wk", "wv")], axis=1)).astype(ml_dtypes.bfloat16),
        "woT": np.ascontiguousarray(
            np.asarray(inputs["L0_attn_wo"], np.float32).T).astype(
                ml_dtypes.bfloat16),
        "ws1T": np.ascontiguousarray(
            np.asarray(inputs["L0_ffn_Ws1"], np.float32).T).astype(
                ml_dtypes.bfloat16),
        "ws3T": np.ascontiguousarray(
            np.asarray(inputs["L0_ffn_Ws3"], np.float32).T).astype(
                ml_dtypes.bfloat16),
        "ws2T": np.ascontiguousarray(
            np.asarray(inputs["L0_ffn_Ws2"], np.float32).T).astype(
                ml_dtypes.bfloat16),
        "c_gwT": gwT_packed,
        "c_gateb": np.tile(gb[None, :], (128, 1)).astype(np.float32),
    }
    for k, v in consts.items():
        shared["c_" + k] = v
    in_maps = []
    for core in range(c.NC):
        m = dict(shared)
        m["x_loc"] = np.ascontiguousarray(x[core * c.TL:(core + 1) * c.TL])
        eids = [core, core + c.NC]
        w1p = W1[eids].reshape(2, c.IC, 128, c.DC, 128) \
            .transpose(0, 1, 4, 3, 2).reshape(2, c.IC, 128, c.D)
        w3p = W3[eids].reshape(2, c.IC, 128, c.DC, 128) \
            .transpose(0, 1, 4, 3, 2).reshape(2, c.IC, 128, c.D)
        m["w13p"] = np.ascontiguousarray(
            np.concatenate([w1p, w3p], axis=-1)).astype(ml_dtypes.bfloat16)
        m["w2q"] = np.ascontiguousarray(
            W2[eids].reshape(2, c.DC, 128, c.IC, 128)
            .transpose(0, 1, 4, 3, 2).reshape(2, c.DC, 128, c.I)) \
            .astype(ml_dtypes.bfloat16)
        sel = np.zeros((128, 2 * c.E), np.float32)
        sel[:, eids[0]] = 1.0
        sel[:, c.E + eids[1]] = 1.0
        m["c_selAB"] = sel
        cm = np.zeros((1, 8), np.int32)
        cm[0, 0] = core
        m["cmeta"] = cm
        in_maps.append(m)
    return in_maps


def run_cfg(c, inputs, dbg=(), want_raw=False):
    nc = build(c, dbg=dbg)
    in_maps = make_inmaps(c, inputs)
    res = run_bass_kernel_spmd(nc, in_maps, core_ids=list(range(c.NC)))
    out = np.concatenate([r["out_loc"] for r in res.results], axis=0)
    out = out.reshape(c.B, c.S, c.D)
    if want_raw or dbg:
        return out, res.results
    return out


def kernel(**inputs):
    return run_cfg(cfg_full(), inputs)
